# version 23
# baseline (speedup 1.0000x reference)
import math
import numpy as np
import ml_dtypes
import concourse.bass as bass
import concourse.mybir as mybir
from concourse.bass_utils import run_bass_kernel_spmd

F32 = mybir.dt.float32
BF16 = mybir.dt.bfloat16
U8 = mybir.dt.uint8
AF = mybir.ActivationFunctionType
ALU = mybir.AluOpType
AX = mybir.AxisListType.X

DM = 2048
NCH = 16
S = 2048
TQ = 1024
QT = 8
HEADS = 16
DH = 128
ICH = 64
IGR = 8
RS128 = 1.0 / math.sqrt(128.0)
EPS = 1e-5


def _f32_layout():
    off, d = 0, {}

    def add(name, elems):
        nonlocal off
        d[name] = off
        off += elems

    add("bqkv", 128 * 48)
    add("bo_row", 2048)
    add("gs", 128 * 64)
    add("us", 128 * 64)
    add("ds_row", 2048)
    add("masks", 128 * 2048)
    for h in range(HEADS):
        add(f"wqkv{h}", 128 * 6144)
    for dh in range(2):
        for c in range(NCH):
            add(f"wo{dh}_{c}", 128 * 1024)
    d["_total"] = off
    return d


F32OFF = _f32_layout()
U8_CHUNK = 3 * 128 * 2048
U8_TOTAL = ICH * U8_CHUNK


def _q_rows(w):
    tiles = np.arange(QT) * 2 + w
    return (tiles[:, None] * 128 + np.arange(128)[None, :]).reshape(-1)


def _build_masks(w):
    m = np.zeros((128, QT, 256), dtype=np.float32)
    r = np.arange(128)[:, None]
    i = np.arange(128)[None, :]
    diag = np.where(i > r, np.float32(-10000.0), np.float32(0.0))
    for j in range(QT):
        if w == 0:
            m[:, j, 0:128] = diag
            m[:, j, 128:256] = -10000.0
        else:
            m[:, j, 128:256] = diag
    return m.reshape(128, QT * 256)


def pack_host(inputs):
    x = np.ascontiguousarray(np.asarray(inputs["L0_attn_x"], dtype=np.float32))
    wqkv = np.ascontiguousarray(np.asarray(inputs["L0_attn_Wqkv"], dtype=np.float32))
    bqkv = np.asarray(inputs["L0_attn_bqkv"], dtype=np.float32)
    wo = np.ascontiguousarray(np.asarray(inputs["L0_attn_Wo"], dtype=np.float32))
    bo = np.asarray(inputs["L0_attn_bo"], dtype=np.float32)
    gw = np.ascontiguousarray(np.asarray(inputs["L0_ffn_gate_w"], dtype=np.int32))
    gsc = np.asarray(inputs["L0_ffn_gate_scale"], dtype=np.float32)
    uw = np.ascontiguousarray(np.asarray(inputs["L0_ffn_up_w"], dtype=np.int32))
    usc = np.asarray(inputs["L0_ffn_up_scale"], dtype=np.float32)
    dw = np.ascontiguousarray(np.asarray(inputs["L0_ffn_down_w"], dtype=np.int32))
    dsc = np.asarray(inputs["L0_ffn_down_scale"], dtype=np.float32)

    blob = np.empty(F32OFF["_total"], dtype=np.float32)

    def put(name, arr):
        a = np.ascontiguousarray(arr, dtype=np.float32).reshape(-1)
        blob[F32OFF[name]:F32OFF[name] + a.size] = a

    put("bqkv", bqkv.reshape(3, 16, 128).transpose(2, 1, 0))
    put("bo_row", bo)
    put("gs", gsc.reshape(64, 128).T)
    put("us", usc.reshape(64, 128).T)
    put("ds_row", dsc)
    wq5 = wqkv.reshape(NCH, 128, 3, HEADS, DH)
    for h in range(HEADS):
        qkv = np.stack([wq5[:, :, t, h, :] for t in range(3)], axis=0)
        put(f"wqkv{h}", qkv.transpose(2, 0, 1, 3))
    wo4 = wo.reshape(NCH, 128, 2, 1024)
    for dh in range(2):
        for c in range(NCH):
            put(f"wo{dh}_{c}", wo4[c, :, dh, :])

    g8 = gw.view(np.uint8).reshape(8192, 2048, 4)[:, :, 0]
    u8 = uw.view(np.uint8).reshape(8192, 2048, 4)[:, :, 0]
    d8 = dw.view(np.uint8).reshape(2048, 8192, 4)[:, :, 0]
    gt = g8.reshape(ICH, 128, NCH, 128).transpose(0, 3, 2, 1).reshape(ICH, 128, 2048)
    ut = u8.reshape(ICH, 128, NCH, 128).transpose(0, 3, 2, 1).reshape(ICH, 128, 2048)
    dt = d8.reshape(2048, ICH, 128).transpose(1, 2, 0)
    ublob = np.ascontiguousarray(np.stack([gt, ut, dt], axis=2)).reshape(-1)

    fblobs = []
    for w in range(2):
        fb = blob.copy()
        fb[F32OFF["masks"]:F32OFF["masks"] + 128 * 2048] = \
            _build_masks(w).reshape(-1)
        fblobs.append(fb)

    cores = []
    for c in range(8):
        seq, w = c // 2, c % 2
        cores.append(dict(
            x_full=np.ascontiguousarray(x[seq]),
            x_q=np.ascontiguousarray(x[seq][_q_rows(w)]),
            f32_blob=fblobs[w],
            u8_blob=ublob,
        ))
    return cores


def _evs(rel):
    if rel is None:
        return []
    if isinstance(rel, tuple):
        return [rel]
    return [e for r in rel for e in _evs(r)]


class Plan:
    ENGS = ("sync", "gpsimd", "vector", "scalar", "tensor")

    def __init__(self, nc):
        self.nc = nc
        self.ops = {e: [] for e in self.ENGS}
        self.sems = {}
        self.cnt = {}
        self.lastw = {}

    def sem(self, name):
        if name not in self.sems:
            self.sems[name] = self.nc.alloc_semaphore(name)
            self.cnt[name] = 0
        return self.sems[name]

    def wait(self, eng, rel):
        for sem, val in _evs(rel):
            if val is None or val <= 0:
                continue
            self.sem(sem)
            assert val <= self.cnt[sem], \
                f"wait {eng} {sem}>={val} exceeds planned {self.cnt[sem]}"
            key = (eng, sem)
            if self.lastw.get(key, 0) >= val:
                continue
            self.lastw[key] = val
            s = self.sems[sem]
            self.ops[eng].append(lambda e, s=s, v=val: e.wait_ge(s, v))

    def op(self, eng, meth, *args, inc=None, dma=False, **kw):
        n = 16 if dma else 1
        ev, s = None, None
        if inc is not None:
            self.sem(inc)
            self.cnt[inc] += n
            ev = (inc, self.cnt[inc])
            s = self.sems[inc]

        def f(e, meth=meth, args=args, kw=kw, s=s, n=n):
            r = getattr(e, meth)(*args, **kw)
            if s is not None:
                r.then_inc(s, n)
        self.ops[eng].append(f)
        return ev

    def build(self):
        with self.nc.Block() as block:
            for eng in self.ENGS:
                ops = self.ops[eng]
                getattr(block, eng)(lambda e, ops=ops: [f(e) for f in ops])


def build_nc(debug_taps=()):
    nc = bass.Bass("TRN2", target_bir_lowering=False)
    x_hbm = nc.dram_tensor("x_full", [S, DM], F32, kind="ExternalInput")
    xq_hbm = nc.dram_tensor("x_q", [TQ, DM], F32, kind="ExternalInput")
    fb_hbm = nc.dram_tensor("f32_blob", [F32OFF["_total"]], F32,
                            kind="ExternalInput")
    ub_hbm = nc.dram_tensor("u8_blob", [U8_TOTAL], U8, kind="ExternalInput")
    out_hbm = nc.dram_tensor("out", [TQ, DM], F32, kind="ExternalOutput")

    ident_hbm = nc.inline_tensor(np.eye(128, dtype=ml_dtypes.bfloat16), "identc")
    ones_hbm = nc.inline_tensor(np.ones((1, 128), dtype=np.float32), "onesc")

    def fpc(name, count):
        o = F32OFF[name]
        return fb_hbm[o:o + count]

    def row2d(ap, p):
        return ap.rearrange("(p n) -> p n", p=p)

    KB = 1024
    base = (nc.sbuf_base + 127) // 128 * 128

    def at(name, shape, dtype, off_kb):
        off = base + int(off_kb * KB)
        esz = {F32: 4, BF16: 2, U8: 1}[dtype]
        end = off + int(np.prod(shape[1:])) * esz
        assert end <= nc.sbuf_top, f"{name}: {end} > {nc.sbuf_top}"
        return nc.alloc_sbuf_tensor_at(name, shape, dtype, offset=off)

    IDENT = at("IDENT", [128, 128], BF16, 0)
    ONES = at("ONES", [1, 128], F32, 0.25)
    BQKV = at("BQKV", [128, 48], F32, 0.75)
    BOBC = at("BOBC", [128, 2048], F32, 1)
    DSBC = at("DSBC", [128, 2048], F32, 9)
    MASKS = at("MASKS", [128, 2048], BF16, 17)
    GSC = at("GSC", [128, 64], F32, 21)
    USC = at("USC", [128, 64], F32, 21.25)
    STAT = at("STAT", [128, 128], F32, 21.5)
    QBSB = at("QBSB", [128, 16], F32, 206.5)
    XHF = at("XHF", [128, NCH * 2048], BF16, 22)
    YS = at("YS", [128, QT * 2048], F32, 22)
    XQF = at("XQF", [128, NCH * 1024], BF16, 86)
    SIG = at("SIG", [128, 1024], F32, 86)
    HBUF = at("HBUF", [128, IGR * 1024], BF16, 90)
    GUD8 = [at(f"GUD8{i}", [128, 3 * 2048], U8, 106 + 6 * i) for i in range(2)]
    OS = at("OS", [128, HEADS * 1024], BF16, 118)
    YF = at("YF", [128, NCH * 1024], BF16, 118)
    BOROW = at("BOROW", [1, 2048], F32, 150)
    DSROW = at("DSROW", [1, 2048], F32, 158)
    XT = [at(f"XT{i}", [128, 2048], F32, 150 + 8 * i) for i in range(2)]
    XTM = [at(f"XTM{i}", [128, 2048], BF16, 166 + 4 * i) for i in range(2)]
    WQKVB = [at("WQKVB0", [128, 6144], BF16, 178.5),
             at("WQKVB1", [128, 6144], BF16, 150)]
    KS = [at("KS0", [128, 2048], BF16, 190.5), at("KS1", [128, 2048], BF16, 162)]
    VS = [at("VS0", [128, 2048], BF16, 194.5), at("VS1", [128, 2048], BF16, 166)]
    QH = [at("QH0", [128, 1024], BF16, 174.5),
          at("QH1", [128, 1024], BF16, 176.5)]
    VF = [at(f"VF{i}", [128, 512], BF16, 198.5 + 1.0 * i) for i in range(2)]
    PT = [at("PT", [128, 2048], BF16, 170),
          at("PTB", [128, 2048], BF16, 200.5)]
    PS = at("PS", [128, 1024], BF16, 204.5)
    WOC = [at("WOC0", [128, 4096], BF16, 150),
           at("WOC1", [128, 4096], BF16, 170)]
    YTM = [at(f"YTM{i}", [128, 2048], BF16, 162 + 4 * i) for i in range(2)]
    GD = at("GD", [128, 2048], BF16, 150)
    UD = at("UD", [128, 2048], BF16, 154)
    DDG = at("DDG", [128, IGR * 2048], BF16, 158)

    SC = nc.alloc_psum_tensor("SC", [128, 2048], F32)
    psum_mark = nc.psum_base
    OP = nc.alloc_psum_tensor("OP", [128, 128], F32)
    TP = [nc.alloc_psum_tensor(f"TP{i}", [128, 1024], BF16)
          for i in range(2)]
    PJP = nc.alloc_psum_tensor("PJP", [128, 512], F32)

    p = Plan(nc)
    ST, SV, SS, SX, SW, SO = "st", "sv", "ss", "sx", "sw", "so"
    taps = []
    EPSC = STAT[:, 127:128]
    p.op("vector", "memset", EPSC, EPS)
    QBS = QBSB[:, :]

    tap_evs = []

    def tap(name, buf, shape, dtype, wait_evs):
        if name not in debug_taps:
            return
        d = nc.dram_tensor(f"dbg_{name}", shape, dtype, kind="ExternalOutput")
        p.wait("gpsimd", wait_evs)
        if tap_evs:
            p.wait("gpsimd", tap_evs[-1])
        tap_evs.append(p.op("gpsimd", "dma_start", out=d[:, :], in_=buf,
                            inc=SO, dma=True))
        taps.append(f"dbg_{name}")

    ev_c0 = None
    for dst, src in ((IDENT, ident_hbm), (ONES, ones_hbm)):
        p.wait("gpsimd", ev_c0)
        ev_c0 = p.op("gpsimd", "dma_start", out=dst[:, :], in_=src[:, :],
                     inc="swc", dma=True)
    for dst, nm, np_ in ((BQKV, "bqkv", 128), (BOROW, "bo_row", 1),
                         (GSC, "gs", 128), (USC, "us", 128),
                         (DSROW, "ds_row", 1), (MASKS, "masks", 128)):
        cnt = int(np.prod(dst.shape))
        p.wait("gpsimd", ev_c0)
        ev_c0 = p.op("gpsimd", "dma_start", out=dst[:, :],
                     in_=row2d(fpc(nm, cnt), np_), inc="swc", dma=True)
    consts_ev = ev_c0

    p.wait("tensor", consts_ev)
    bcast_st = None
    sc_free = None
    for src, dst in ((BOROW, BOBC), (DSROW, DSBC)):
        p.wait("tensor", sc_free)
        for d in range(4):
            bcast_st = p.op("tensor", "matmul", SC[:, d * 512:(d + 1) * 512],
                            ONES[:, :], src[:, d * 512:(d + 1) * 512],
                            start=True, stop=True,
                            inc=ST if d == 3 else None)
        p.wait("vector", bcast_st)
        sc_free = p.op("vector", "tensor_copy", dst[:, :], SC[:, :], inc=SV)
    p.wait("vector", consts_ev)
    qbs_ev = p.op("vector", "tensor_scalar_mul", QBS,
                  BQKV[:, 0:48].rearrange("p (h t) -> p h t", t=3)[:, :, 0],
                  RS128, inc=SV)

    def ln_stats(src_ap, statbase, ready_ev):
        bst = STAT[:, statbase:statbase + 24]
        mv = STAT[:, statbase + 24:statbase + 26]
        sd = STAT[:, statbase + 26:statbase + 27]
        rstd = STAT[:, statbase + 27:statbase + 28]
        p.wait("vector", ready_ev)
        ev_b = None
        for q in range(4):
            ev_b = p.op("vector", "bn_stats", bst[:, 6 * q:6 * q + 6],
                        src_ap[:, q * 512:(q + 1) * 512],
                        inc=SV if q == 3 else None)
        p.wait("vector", ev_b)
        ev_a = p.op("vector", "bn_aggr", mv, bst, inc=SV)
        p.wait("scalar", ev_a)
        ev_sd = p.op("scalar", "activation", sd, mv[:, 1:2], AF.Sqrt,
                     bias=EPSC, inc=SS)
        p.wait("vector", ev_sd)
        ev_ln = p.op("vector", "reciprocal", rstd, sd, inc=SV)
        return mv[:, 0:1], rstd, ev_ln

    tp_rel = [None, None]
    tp_ctr = [0]

    def transpose16(src_tile, dst, dst_col, dst_stride, norm_ev, tp_bufs,
                    tp_rel_l, copies="alt"):
        ev_t = None
        ev_c = []
        dst3 = dst[:, :].rearrange("p (c n) -> p c n", n=dst_stride)
        for g in range(4):
            b = tp_ctr[0] % 2
            tp_ctr[0] += 1
            p.wait("tensor", [norm_ev, tp_rel_l[b]])
            for cc in range(4):
                c = 4 * g + cc
                ev_t = p.op("tensor", "transpose",
                            tp_bufs[b][:, cc * 128:(cc + 1) * 128],
                            src_tile[:, c * 128:(c + 1) * 128], IDENT[:, :],
                            inc=ST if cc == 3 else None)
            ceng = "scalar" if copies == "scalar" or b == 1 else "vector"
            p.wait(ceng, ev_t)
            ev = p.op(ceng, "copy" if ceng == "scalar" else "tensor_copy",
                      dst3[:, 4 * g:4 * g + 4, dst_col:dst_col + 128],
                      tp_bufs[b][:, 0:512].rearrange("p (c n) -> p c n", n=128),
                      inc=SS if ceng == "scalar" else SV)
            tp_rel_l[b] = ev
            if g >= 2:
                ev_c.append(ev)
        return ev_t, ev_c

    ev_wqkv = {}
    ks_ready = {}
    vs_ready = {}
    q_ready = {}
    ks_rel = [None, None]
    vs_rel = [None, None]
    qh_rel = [None, None]
    vf_rel = [None, None]
    pjp_rel = None
    wqkv_extra = [[], None]
    xhf_tile_ev = {}
    xqf_done = None
    proj_last_mm = [None, None]

    def emit_wqkv_dma(hh):
        par = hh % 2
        p.wait("gpsimd", [proj_last_mm[par], wqkv_extra[par]])
        ev_wqkv[hh] = p.op("gpsimd", "dma_start", out=WQKVB[par][:, :],
                           in_=row2d(fpc(f"wqkv{hh}", 128 * 6144), 128),
                           inc=f"sw{par}", dma=True)

    def emit_proj_unit(hh, u):
        nonlocal vf_rel, pjp_rel
        par = hh % 2
        if u < 4:
            sg = u
            p.wait("tensor", [ev_wqkv[hh], pjp_rel,
                              xhf_tile_ev.get(4 * sg + 3)])
            ev_mm = None
            for c in range(NCH):
                ev_mm = p.op("tensor", "matmul", PJP[:, :],
                             WQKVB[par][:, 2048 + c * 128: 2048 + (c + 1) * 128],
                             XHF[:, c * 2048 + sg * 512: c * 2048 + sg * 512 + 512],
                             start=(c == 0), stop=(c == NCH - 1),
                             inc=ST if c == NCH - 1 else None)
            proj_last_mm[par] = ev_mm
            p.wait("scalar", [ev_mm, ks_rel[par]])
            ev = p.op("scalar", "activation",
                      KS[par][:, sg * 512:(sg + 1) * 512], PJP[:, :],
                      AF.Identity, bias=BQKV[:, 3 * hh + 1:3 * hh + 2], inc=SS)
            pjp_rel = ev
            if sg == 3:
                ks_ready[hh] = ev
        elif u < 8:
            sg = u - 4
            vslot = sg % 2
            p.wait("tensor", [ev_wqkv[hh], pjp_rel,
                              xhf_tile_ev.get(4 * sg + 3)])
            ev_mm = None
            for c in range(NCH):
                ev_mm = p.op("tensor", "matmul", PJP[:, :],
                             WQKVB[par][:, 4096 + c * 128: 4096 + (c + 1) * 128],
                             XHF[:, c * 2048 + sg * 512: c * 2048 + sg * 512 + 512],
                             start=(c == 0), stop=(c == NCH - 1),
                             inc=ST if c == NCH - 1 else None)
            proj_last_mm[par] = ev_mm
            p.wait("scalar", [ev_mm, vf_rel[vslot]])
            ev = p.op("scalar", "activation",
                      VF[vslot][:, :], PJP[:, :],
                      AF.Identity, bias=BQKV[:, 3 * hh + 2:3 * hh + 3], inc=SS)
            pjp_rel = ev
            b = tp_ctr[0] % 2
            tp_ctr[0] += 1
            p.wait("tensor", [ev, tp_rel[b]])
            ev_vt = None
            for q4 in range(4):
                ev_vt = p.op("tensor", "transpose",
                             TP[b][:, q4 * 128:(q4 + 1) * 128],
                             VF[vslot][:, q4 * 128:(q4 + 1) * 128],
                             IDENT[:, :], inc=ST if q4 == 3 else None)
            ceng = "vector" if b == 0 else "scalar"
            p.wait(ceng, [ev_vt, vs_rel[par]])
            ev_c = p.op(ceng, "tensor_copy" if b == 0 else "copy",
                        VS[par][:, sg * 512:(sg + 1) * 512],
                        TP[b][:, 0:512], inc=SV if b == 0 else SS)
            tp_rel[b] = ev_c
            vf_rel[vslot] = ev_vt
            if sg == 3:
                vs_ready[hh] = ev_c
        else:
            tg = u - 8
            p.wait("tensor", [ev_wqkv[hh], pjp_rel, xqf_done])
            ev_mm = None
            for c in range(NCH):
                ev_mm = p.op("tensor", "matmul", PJP[:, :],
                             WQKVB[par][:, c * 128:(c + 1) * 128],
                             XQF[:, c * 1024 + tg * 512: c * 1024 + tg * 512 + 512],
                             start=(c == 0), stop=(c == NCH - 1),
                             inc=ST if c == NCH - 1 else None)
            proj_last_mm[par] = ev_mm
            p.wait("scalar", [ev_mm, qh_rel[par], qbs_ev])
            ev = p.op("scalar", "activation",
                      QH[par][:, tg * 512:(tg + 1) * 512], PJP[:, :],
                      AF.Identity, bias=QBS[:, hh:hh + 1], scale=RS128,
                      inc=SS)
            pjp_rel = ev
            if tg == 1:
                q_ready[hh] = ev

    emit_wqkv_dma(0)
    wo_bf = nc.dram_tensor("wo_bf", [2 * NCH * 128 * 1024], BF16,
                           kind="Internal")
    ev_wobf = p.op("gpsimd", "dma_start",
                   out=wo_bf[:].rearrange("(p n) -> p n", p=2048),
                   in_=fpc("wo0_0", 2 * NCH * 128 * 1024).rearrange(
                       "(p n) -> p n", p=2048),
                   inc="swo", dma=True)

    xt_rel = [bcast_st, bcast_st]
    xtm_rel = [None, None]
    xt_readers = [None, None]
    p.wait("tensor", consts_ev)
    weave = {3: [0], 5: [4], 7: [1], 9: [5], 11: [2], 13: [6], 15: [3, 7]}
    tiles = [("f", i) for i in range(NCH)] + [("q", i) for i in range(QT)]
    pend = None

    def do_transposes(job):
        nonlocal xqf_done
        kind, i, par, ev_norm = job
        if kind == "q":
            ev_t, ev_c = transpose16(XTM[par], XQF, i * 128, 1024, ev_norm,
                                     TP, tp_rel, copies="scalar")
            xqf_done = ev_c
        else:
            ev_t, ev_c = transpose16(XTM[par], XHF, i * 128, 2048, ev_norm,
                                     TP, tp_rel, copies="scalar")
            xhf_tile_ev[i] = ev_c
            for u in weave.get(i, []):
                emit_proj_unit(0, u)
        xtm_rel[par] = ev_t

    for k, (kind, i) in enumerate(tiles):
        par = k % 2
        p.wait("sync", xt_rel[par])
        src_hbm = xq_hbm if kind == "q" else x_hbm
        ev_x = p.op("sync", "dma_start", out=XT[par][:, :],
                    in_=src_hbm[i * 128:(i + 1) * 128, :], inc=f"sx{par}",
                    dma=True)
        mean, rstd, ev_ln = ln_stats(XT[par][:, :], par * 32, ev_x)
        p.wait("vector", [ev_ln, xtm_rel[par]])
        ev_norm = p.op("vector", "tensor_scalar", XTM[par][:, :], XT[par][:, :],
                       mean, rstd, op0=ALU.subtract, op1=ALU.mult, inc=SV)
        xt_rel[par] = [ev_norm]
        xt_readers[par] = [ev_norm]
        if pend is not None:
            do_transposes(pend)
        pend = (kind, i, par, ev_norm)
    do_transposes(pend)
    for u in (8, 9):
        emit_proj_unit(0, u)
    x_done = xt_readers[0] + xt_readers[1] + xtm_rel
    xhf_done = xhf_tile_ev[NCH - 1]
    wqkv_extra[1] = x_done

    tap("XHF", XHF[:, :], [128, NCH * 2048], BF16, [xhf_done])
    tap("XQF", XQF[:, :], [128, NCH * 1024], BF16, [xqf_done])

    sc_rel = None
    sc_half_rel = [None, None]
    os_rel = None
    ps_rel = [None, None]
    pt_rel = [None, None]
    attn_last = None
    pend_tile = [None]

    def emit_tav():
        nonlocal attn_last, os_rel
        if pend_tile[0] is None:
            return
        h, par, j, nb, pbuf, ev_pn = pend_tile[0]
        pend_tile[0] = None
        ev_av = None
        ev_tp_last = None
        pend_av = []

        def emit_av():
            nonlocal ev_av
            g0, gn, bk, ev_psc = pend_av.pop(0)
            p.wait("tensor", ev_psc)
            if g0 == 0:
                p.wait("tensor", [os_rel, vs_ready[h]])
            for cc in range(gn):
                b = g0 + cc
                ev_av = p.op("tensor", "matmul", OP[:, :],
                             VS[par][:, b * 128:(b + 1) * 128],
                             PS[:, bk * 512 + cc * 128:
                                 bk * 512 + (cc + 1) * 128],
                             start=(b == 0), stop=(b == nb - 1),
                             inc=ST if cc == gn - 1 else None)
            ps_rel[bk] = ev_av

        for g0 in range(0, nb, 4):
            gn = min(4, nb - g0)
            bk = tp_ctr[0] % 2
            tp_ctr[0] += 1
            p.wait("tensor", [ev_pn, tp_rel_tp[bk]])
            ev_tp = None
            for cc in range(gn):
                b = g0 + cc
                ev_tp = p.op("tensor", "transpose",
                             TP[bk][:, cc * 128:(cc + 1) * 128],
                             PT[pbuf][:, b * 128:(b + 1) * 128], IDENT[:, :],
                             inc=ST if cc == gn - 1 else None)
            ev_tp_last = ev_tp
            if g0 >= 4:
                emit_av()
            ceng = "vector" if bk == 0 else "scalar"
            p.wait(ceng, [ev_tp, ps_rel[bk]])
            ev_psc = p.op(ceng, "tensor_copy" if bk == 0 else "copy",
                          PS[:, bk * 512:bk * 512 + gn * 128],
                          TP[bk][:, 0:gn * 128], inc=SV if bk == 0 else SS)
            tp_rel_tp[bk] = ev_psc
            pend_av.append((g0, gn, bk, ev_psc))
        emit_av()
        pt_rel[pbuf] = ev_tp_last
        vs_rel[par] = ev_av
        attn_last = ev_av
        p.wait("scalar", ev_av)
        os_rel = p.op("scalar", "activation",
                      OS[:, h * 1024 + j * 128: h * 1024 + (j + 1) * 128],
                      OP[:, :], AF.Copy, inc=SS)

    tp_rel_tp = tp_rel
    ev_xq = []
    tail_groups = [(ci, dh, cg, j2, d)
                   for ci, (dh, cg) in enumerate([(0, 0), (0, 1),
                                                  (1, 0), (1, 1)])
                   for j2 in range(QT) for d in range(2)]
    tail_ev = {}
    tw_dma = [None] * 4
    tw_state = {"last_mm": None, "idx": 0, "os": None}

    def emit_tail_wo():
        nonlocal pjp_rel
        ci, dh, cg, j2, d = tail_groups[tw_state["idx"]]
        tw_state["idx"] += 1
        if tw_dma[ci] is None:
            p.wait("sync", [proj_last_mm[1] if ci == 0
                            else tw_state["last_mm"], ev_wobf])
            tw_dma[ci] = p.op(
                "sync", "dma_start",
                out=WOC[0][:, :].rearrange("p (c n) -> p c n", c=4),
                in_=wo_bf[((dh * NCH + cg * 4) * 128) * 1024:
                          ((dh * NCH + cg * 4 + 4) * 128) * 1024
                          ].rearrange("(c p n) -> p c n", c=4, p=128),
                inc="sx0", dma=True)
        p.wait("tensor", [tw_dma[ci], tw_state["os"], pjp_rel])
        ev_mm = None
        for k4 in range(4):
            c = cg * 4 + k4
            ev_mm = p.op("tensor", "matmul", PJP[:, :],
                         OS[:, c * 1024 + j2 * 128: c * 1024 + (j2 + 1) * 128],
                         WOC[0][:, k4 * 1024 + d * 512:
                                k4 * 1024 + d * 512 + 512],
                         start=(k4 == 0), stop=(k4 == 3),
                         inc=ST if k4 == 3 else None)
        tw_state["last_mm"] = ev_mm
        ysl = YS[:, j2 * 2048 + dh * 1024 + d * 512:
                 j2 * 2048 + dh * 1024 + d * 512 + 512]
        p.wait("vector", [ev_mm, ev_xq[j2]])
        ev = p.op("vector", "scalar_tensor_tensor", ysl, PJP[:, :], 0.0, ysl,
                  op0=ALU.add, op1=ALU.add, inc=SV)
        pjp_rel = ev
        tail_ev.setdefault((j2, dh), []).append(ev)

    for h in range(HEADS):
        par = h % 2
        if h + 1 < HEADS:
            emit_wqkv_dma(h + 1)
        if h == HEADS - 1:
            p.wait("gpsimd", [proj_last_mm[0], proj_last_mm[1]])
            p.wait("scalar", [proj_last_mm[0], proj_last_mm[1]])
            for j2 in range(QT):
                qeng = "gpsimd" if j2 % 2 == 0 else "scalar"
                if j2 > 1:
                    p.wait(qeng, ev_xq[-2])
                ev_xq.append(p.op(qeng, "dma_start",
                                  out=YS[:, j2 * 2048:(j2 + 1) * 2048],
                                  in_=xq_hbm[j2 * 128:(j2 + 1) * 128, :],
                                  inc="sxq" if j2 % 2 == 0 else "sxq2",
                                  dma=True))
            tw_state["os"] = os_rel
        nq = 10 if h + 1 < HEADS else 0
        for j in range(QT):
            ext = 256 * (j + 1)
            nb = 2 * (j + 1)
            sl = 112 + 4 * (j % 2)
            mneg = STAT[:, sl:sl + 1]
            ssum = STAT[:, sl + 1:sl + 2]
            rinv = STAT[:, sl + 2:sl + 3]
            if j <= 3:
                off = (j % 2) * 1024
                p.wait("tensor", [sc_half_rel[j % 2], ks_ready[h],
                                  q_ready[h]])
            else:
                off = 0
                p.wait("tensor", [sc_half_rel[0], sc_half_rel[1], sc_rel,
                                  ks_ready[h], q_ready[h]])
            SCV = SC[:, off:off + ext]
            ev_qk = None
            for seg in range(0, ext, 512):
                seg_w = min(512, ext - seg)
                ev_qk = p.op("tensor", "matmul", SCV[:, seg:seg + seg_w],
                             QH[par][:, j * 128:(j + 1) * 128],
                             KS[par][:, seg:seg + seg_w],
                             start=True, stop=True,
                             inc=ST if seg + seg_w >= ext else None)
            ks_rel[par] = ev_qk
            qh_rel[par] = ev_qk
            p.wait("vector", ev_qk)
            ev_msk = p.op("vector", "tensor_tensor", SCV[:, ext - 256:ext],
                          SCV[:, ext - 256:ext], MASKS[:, j * 256:(j + 1) * 256],
                          ALU.add, inc=SV)
            p.wait("vector", ev_msk)
            ev_mx = p.op("vector", "reduce_max", mneg, SCV[:, 0:ext], axis=AX,
                         negate=True, inc=SV)
            pbuf = (h * QT + j) % 2
            p.wait("scalar", [ev_mx, pt_rel[pbuf]])
            ev_exp = p.op("scalar", "activation", PT[pbuf][:, 0:ext],
                          SCV[:, 0:ext],
                          AF.Exp, bias=mneg, scale=1.0, accum_out=ssum,
                          inc=SS)
            if j <= 3:
                sc_half_rel[j % 2] = ev_exp
            else:
                sc_rel = ev_exp
            p.wait("vector", ev_exp)
            ev_ri = p.op("vector", "reciprocal", rinv, ssum, inc=SV)
            p.wait("vector", ev_ri)
            ev_pn = p.op("vector", "tensor_scalar_mul", PT[pbuf][:, 0:ext],
                         PT[pbuf][:, 0:ext], rinv, inc=SV)
            lo = (nq * j) // QT
            hi = (nq * (j + 1)) // QT
            for u in range(lo, hi):
                emit_proj_unit(h + 1, u)
            if h == HEADS - 1:
                for _ in range(8):
                    if tw_state["idx"] < len(tail_groups):
                        emit_tail_wo()
            emit_tav()
            pend_tile[0] = (h, par, j, nb, pbuf, ev_pn)
    emit_tav()
    os_done = os_rel
    xhf_rel = [proj_last_mm[0], proj_last_mm[1]]

    tap("OS", OS[:, :], [128, HEADS * 1024], BF16, [os_done])

    nc.psum_base = psum_mark
    TP2 = [nc.alloc_psum_tensor(f"TP2_{i}", [128, 1024], BF16)
           for i in range(2)]
    tp_rel2 = [[os_done] + tp_rel + [pjp_rel], [os_done] + tp_rel + [pjp_rel]]

    p.wait("sync", list(tap_evs))
    p.wait("gpsimd", list(tap_evs))

    pre_u8 = [None, None]
    for par8 in range(2):
        if par8 == 0:
            p.wait("scalar", xhf_rel)
        pre_u8[par8] = p.op(
            "scalar", "dma_start", out=GUD8[par8][:, :],
            in_=row2d(ub_hbm[par8 * U8_CHUNK:(par8 + 1) * U8_CHUNK], 128),
            inc=f"su{par8}", dma=True)

    woc_rel = [[tw_state["last_mm"]], [attn_last]]
    pj_rel = [[sc_rel, sc_half_rel[0], sc_half_rel[1]],
              [sc_rel, sc_half_rel[0], sc_half_rel[1]]]
    ev_y = {}
    ytm_rel = [None, None]
    yf_done = None
    for jq in range(4):
        for dh in range(2):
            ev_wo_j = [None, None]
            for c in range(8, NCH):
                cb = (c // 4) % 2
                if c % 4 == 0:
                    qeng = "sync" if cb == 0 else "gpsimd"
                    p.wait(qeng, [woc_rel[cb], ev_wobf])
                    if cb and ev_xq:
                        p.wait(qeng, ev_xq[-1])
                    ev_woc = p.op(
                        qeng, "dma_start",
                        out=WOC[cb][:, :].rearrange("p (c n) -> p c n", c=4),
                        in_=wo_bf[((dh * NCH + c) * 128) * 1024:
                                  ((dh * NCH + c + 4) * 128) * 1024
                                  ].rearrange("(c p n) -> p c n", c=4, p=128),
                        inc="sx0" if cb == 0 else "swoc", dma=True)
                p.wait("tensor", ev_woc)
                if c == 8:
                    p.wait("tensor", [os_done, pj_rel[0], pj_rel[1]])
                ev_mm = None
                for jl in range(2):
                    j = 2 * jq + jl
                    PJ = SC[:, jl * 1024:(jl + 1) * 1024]
                    for d in range(2):
                        last_c = (c == NCH - 1)
                        do_inc = (last_c and d == 1) or (jl == 1 and d == 1)
                        ev_mm = p.op(
                            "tensor", "matmul", PJ[:, d * 512:(d + 1) * 512],
                            OS[:, c * 1024 + j * 128:
                                c * 1024 + (j + 1) * 128],
                            WOC[cb][:, (c % 4) * 1024 + d * 512:
                                (c % 4) * 1024 + (d + 1) * 512],
                            start=(c == 8), stop=last_c,
                            inc=ST if do_inc else None)
                        if last_c and d == 1:
                            ev_wo_j[jl] = ev_mm
                woc_rel[cb] = ev_mm
            for jl in range(2):
                j = 2 * jq + jl
                PJ = SC[:, jl * 1024:(jl + 1) * 1024]
                ysl = YS[:, j * 2048 + dh * 1024: j * 2048 + dh * 1024 + 1024]
                p.wait("vector", [ev_wo_j[jl], ev_xq[j]]
                       + tail_ev.get((j, dh), []))
                ev0 = p.op("vector", "scalar_tensor_tensor", ysl, PJ, 0.0, ysl,
                           op0=ALU.add, op1=ALU.add, inc=SV)
                p.wait("vector", ev0)
                ev = p.op("vector", "tensor_tensor", ysl, ysl,
                          BOBC[:, dh * 1024: dh * 1024 + 1024], ALU.add,
                          inc=SV)
                pj_rel[jl] = ev
                ev_y[(j, dh)] = ev
        for jl in range(2):
            j = 2 * jq + jl
            par = j % 2
            ytile = YS[:, j * 2048:(j + 1) * 2048]
            ready = [ev_y[(j, 0)], ev_y[(j, 1)]]
            mean, rstd, ev_ln = ln_stats(ytile, 56 + par * 28, ready)
            p.wait("vector", [ev_ln, ytm_rel[par]])
            ev_norm = p.op("vector", "tensor_scalar", YTM[par][:, :], ytile,
                           mean, rstd, op0=ALU.subtract, op1=ALU.mult, inc=SV)
            ev_t, ev_c = transpose16(YTM[par], YF, j * 128, 1024, ev_norm,
                                     TP2, tp_rel2)
            ytm_rel[par] = ev_t
            yf_done = ev_c
    wo_done_t = [woc_rel[0], woc_rel[1]]

    tap("YS_W", YS[:, :], [128, QT * 2048], F32,
        [ev_y[(j, dh)] for j in range(QT) for dh in range(2)])
    tap("YF", YF[:, :], [128, NCH * 1024], BF16, [yf_done])

    nc.psum_base = psum_mark
    PB2 = nc.alloc_psum_tensor("PB2", [128, 2048], F32)
    GP = PB2[:, 0:1024]
    UP = PB2[:, 1024:2048]

    gud_rel = [None, None]
    gd_rel = ud_rel = None
    ddg_rel = None
    sig_rel = None
    hbuf_rel = [None] * IGR
    up_rel = None
    dpsum_rel = {0: [pj_rel[0], pj_rel[1]],
                 1: [yf_done, tp_rel2[0], tp_rel2[1]]}
    ev_h_grp = None
    final_evac = {}

    p.wait("sync", list(tap_evs))
    p.wait("scalar", list(tap_evs))
    p.wait("vector", list(tap_evs))
    for g in range(IGR):
        ev_dd = None
        for cc in range(IGR):
            ic = g * IGR + cc
            par = ic % 2
            if ic < 2:
                ev_u8 = pre_u8[ic]
            else:
                p.wait("sync", gud_rel[par])
                ev_u8 = p.op("sync", "dma_start", out=GUD8[par][:, :],
                             in_=row2d(ub_hbm[ic * U8_CHUNK:
                                              (ic + 1) * U8_CHUNK], 128),
                             inc=f"su{par}", dma=True)
            p.wait("scalar", [ev_u8, gd_rel])
            if ic == 0:
                p.wait("scalar", wo_done_t)
                p.wait("vector", wo_done_t)
            ev_gd = p.op("scalar", "activation", GD[:, :],
                         GUD8[par][:, 0:2048], AF.Copy, bias=-128.0, inc=SS)
            p.wait("scalar", ud_rel)
            ev_ud = p.op("scalar", "activation", UD[:, :],
                         GUD8[par][:, 2048:4096], AF.Copy, bias=-128.0, inc=SS)
            p.wait("vector", [ev_u8, ddg_rel])
            ev_dd = p.op("vector", "scalar_tensor_tensor",
                         DDG[:, cc * 2048:(cc + 1) * 2048],
                         GUD8[par][:, 4096:6144], -128.0, DSBC[:, :],
                         op0=ALU.add, op1=ALU.mult, inc=SV)
            gud_rel[par] = [ev_ud, ev_dd]
            p.wait("tensor", [ev_gd, sig_rel, yf_done, dpsum_rel[1]])
            ev_gm = None
            for c in range(NCH):
                for tg in range(2):
                    last = (c == NCH - 1 and tg == 1)
                    ev_gm = p.op("tensor", "matmul",
                                 GP[:, tg * 512:(tg + 1) * 512],
                                 GD[:, c * 128:(c + 1) * 128],
                                 YF[:, c * 1024 + tg * 512:
                                     c * 1024 + tg * 512 + 512],
                                 start=(c == 0), stop=(c == NCH - 1),
                                 inc=ST if last else None)
            gd_rel = ev_gm
            p.wait("tensor", [ev_ud, up_rel])
            ev_um = None
            for c in range(NCH):
                for tg in range(2):
                    last = (c == NCH - 1 and tg == 1)
                    ev_um = p.op("tensor", "matmul",
                                 UP[:, tg * 512:(tg + 1) * 512],
                                 UD[:, c * 128:(c + 1) * 128],
                                 YF[:, c * 1024 + tg * 512:
                                     c * 1024 + tg * 512 + 512],
                                 start=(c == 0), stop=(c == NCH - 1),
                                 inc=ST if last else None)
            ud_rel = ev_um
            p.wait("scalar", [ev_gm, up_rel])
            sig_rel = p.op("scalar", "activation", SIG[:, :], GP[:, :],
                           AF.Silu, scale=GSC[:, ic:ic + 1], inc=SS)
            p.wait("vector", [sig_rel, ev_um, hbuf_rel[cc]])
            ev_h = p.op("vector", "scalar_tensor_tensor",
                        HBUF[:, cc * 1024:(cc + 1) * 1024], UP[:, :],
                        USC[:, ic:ic + 1], SIG[:, :],
                        op0=ALU.mult, op1=ALU.mult, inc=SV)
            up_rel = ev_h
            if cc == IGR - 1:
                ev_h_grp = ev_h
        if g == 0:
            tap("HBUF0", HBUF[:, :], [128, IGR * 1024], BF16, [ev_h_grp])
            tap("DDG0", DDG[:, :], [128, IGR * 2048], BF16, [ev_dd])
            if debug_taps:
                p.wait("sync", list(tap_evs))
                p.wait("scalar", list(tap_evs))
                p.wait("vector", list(tap_evs))
                p.wait("gpsimd", list(tap_evs))
        ev_dn_last = None
        for j in range(QT):
            DP = SC if j % 2 == 0 else PB2
            p.wait("tensor", [ev_h_grp, ev_dd, dpsum_rel[j % 2]])
            ev_mm = None
            for cc2 in range(IGR):
                for d in range(4):
                    last = (cc2 == IGR - 1 and d == 3)
                    ev_mm = p.op(
                        "tensor", "matmul", DP[:, d * 512:(d + 1) * 512],
                        HBUF[:, cc2 * 1024 + j * 128:
                             cc2 * 1024 + (j + 1) * 128],
                        DDG[:, cc2 * 2048 + d * 512:
                            cc2 * 2048 + d * 512 + 512],
                        start=(cc2 == 0), stop=(cc2 == IGR - 1),
                        inc=ST if last else None)
            ev_dn_last = ev_mm
            p.wait("vector", [ev_mm, dpsum_rel[j % 2],
                              ev_y[(j, 0)], ev_y[(j, 1)]])
            ev_ev = p.op("vector", "tensor_tensor",
                         YS[:, j * 2048:(j + 1) * 2048],
                         YS[:, j * 2048:(j + 1) * 2048], DP[:, :], ALU.add,
                         inc=SV)
            dpsum_rel[j % 2] = ev_ev
            if g == IGR - 1:
                final_evac[j] = ev_ev
        ddg_rel = ev_dn_last
        for cc2 in range(IGR):
            hbuf_rel[cc2] = ev_dn_last

    ev_out = None
    for j in range(QT):
        p.wait("gpsimd", final_evac[j])
        ev_out = p.op("gpsimd", "dma_start",
                      out=out_hbm[j * 128:(j + 1) * 128, :],
                      in_=YS[:, j * 2048:(j + 1) * 2048], inc=SO, dma=True)
    p.wait("gpsimd", ev_out)

    p.build()
    return nc, taps


def kernel(**inputs):
    cores = pack_host(inputs)
    nc, _ = build_nc()
    in_maps = [dict(x_full=c["x_full"], x_q=c["x_q"], f32_blob=c["f32_blob"],
                    u8_blob=c["u8_blob"]) for c in cores]
    res = run_bass_kernel_spmd(nc, in_maps, core_ids=list(range(8)))
    out = np.empty((4, 2048, 2048), dtype=np.float32)
    for c in range(8):
        out[c // 2][_q_rows(c % 2)] = res.results[c]["out"]
    return out


# version 24
# speedup vs baseline: 1.0297x; 1.0297x over previous
import math
import numpy as np
import ml_dtypes
import concourse.bass as bass
import concourse.mybir as mybir
from concourse.bass_utils import run_bass_kernel_spmd

F32 = mybir.dt.float32
BF16 = mybir.dt.bfloat16
U8 = mybir.dt.uint8
AF = mybir.ActivationFunctionType
ALU = mybir.AluOpType
AX = mybir.AxisListType.X

DM = 2048
NCH = 16
S = 2048
TQ = 1024
QT = 8
HEADS = 16
DH = 128
ICH = 64
IGR = 8
RS128 = 1.0 / math.sqrt(128.0)
EPS = 1e-5


def _f32_layout():
    off, d = 0, {}

    def add(name, elems):
        nonlocal off
        d[name] = off
        off += elems

    add("bqkv", 128 * 48)
    add("bo_row", 2048)
    add("gs", 128 * 64)
    add("us", 128 * 64)
    add("ds_row", 2048)
    add("masks", 128 * 2048)
    for h in range(HEADS):
        add(f"wqkv{h}", 128 * 6144)
    for dh in range(2):
        for c in range(NCH):
            add(f"wo{dh}_{c}", 128 * 1024)
    d["_total"] = off
    return d


F32OFF = _f32_layout()
U8_CHUNK = 3 * 128 * 2048
U8_TOTAL = ICH * U8_CHUNK


def _q_rows(w):
    tiles = np.arange(QT) * 2 + w
    return (tiles[:, None] * 128 + np.arange(128)[None, :]).reshape(-1)


def _build_masks(w):
    m = np.zeros((128, QT, 256), dtype=np.float32)
    r = np.arange(128)[:, None]
    i = np.arange(128)[None, :]
    diag = np.where(i > r, np.float32(-10000.0), np.float32(0.0))
    for j in range(QT):
        if w == 0:
            m[:, j, 0:128] = diag
            m[:, j, 128:256] = -10000.0
        else:
            m[:, j, 128:256] = diag
    return m.reshape(128, QT * 256)


def pack_host(inputs):
    x = np.ascontiguousarray(np.asarray(inputs["L0_attn_x"], dtype=np.float32))
    wqkv = np.ascontiguousarray(np.asarray(inputs["L0_attn_Wqkv"], dtype=np.float32))
    bqkv = np.asarray(inputs["L0_attn_bqkv"], dtype=np.float32)
    wo = np.ascontiguousarray(np.asarray(inputs["L0_attn_Wo"], dtype=np.float32))
    bo = np.asarray(inputs["L0_attn_bo"], dtype=np.float32)
    gw = np.ascontiguousarray(np.asarray(inputs["L0_ffn_gate_w"], dtype=np.int32))
    gsc = np.asarray(inputs["L0_ffn_gate_scale"], dtype=np.float32)
    uw = np.ascontiguousarray(np.asarray(inputs["L0_ffn_up_w"], dtype=np.int32))
    usc = np.asarray(inputs["L0_ffn_up_scale"], dtype=np.float32)
    dw = np.ascontiguousarray(np.asarray(inputs["L0_ffn_down_w"], dtype=np.int32))
    dsc = np.asarray(inputs["L0_ffn_down_scale"], dtype=np.float32)

    blob = np.empty(F32OFF["_total"], dtype=np.float32)

    def put(name, arr):
        a = np.ascontiguousarray(arr, dtype=np.float32).reshape(-1)
        blob[F32OFF[name]:F32OFF[name] + a.size] = a

    put("bqkv", bqkv.reshape(3, 16, 128).transpose(2, 1, 0))
    put("bo_row", bo)
    put("gs", gsc.reshape(64, 128).T)
    put("us", usc.reshape(64, 128).T)
    put("ds_row", dsc)
    wq5 = wqkv.reshape(NCH, 128, 3, HEADS, DH)
    for h in range(HEADS):
        qkv = np.stack([wq5[:, :, t, h, :] for t in range(3)], axis=0)
        put(f"wqkv{h}", qkv.transpose(2, 0, 1, 3))
    wo4 = wo.reshape(NCH, 128, 2, 1024)
    for dh in range(2):
        for c in range(NCH):
            put(f"wo{dh}_{c}", wo4[c, :, dh, :])

    g8 = gw.view(np.uint8).reshape(8192, 2048, 4)[:, :, 0]
    u8 = uw.view(np.uint8).reshape(8192, 2048, 4)[:, :, 0]
    d8 = dw.view(np.uint8).reshape(2048, 8192, 4)[:, :, 0]
    gt = g8.reshape(ICH, 128, NCH, 128).transpose(0, 3, 2, 1).reshape(ICH, 128, 2048)
    ut = u8.reshape(ICH, 128, NCH, 128).transpose(0, 3, 2, 1).reshape(ICH, 128, 2048)
    dt = d8.reshape(2048, ICH, 128).transpose(1, 2, 0)
    ublob = np.ascontiguousarray(np.stack([gt, ut, dt], axis=2)).reshape(-1)

    fblobs = []
    for w in range(2):
        fb = blob.copy()
        fb[F32OFF["masks"]:F32OFF["masks"] + 128 * 2048] = \
            _build_masks(w).reshape(-1)
        fblobs.append(fb)

    cores = []
    for c in range(8):
        seq, w = c // 2, c % 2
        cores.append(dict(
            x_full=np.ascontiguousarray(x[seq]),
            x_q=np.ascontiguousarray(x[seq][_q_rows(w)]),
            f32_blob=fblobs[w],
            u8_blob=ublob,
        ))
    return cores


def _evs(rel):
    if rel is None:
        return []
    if isinstance(rel, tuple):
        return [rel]
    return [e for r in rel for e in _evs(r)]


class Plan:
    ENGS = ("sync", "gpsimd", "vector", "scalar", "tensor")

    def __init__(self, nc):
        self.nc = nc
        self.ops = {e: [] for e in self.ENGS}
        self.sems = {}
        self.cnt = {}
        self.lastw = {}

    def sem(self, name):
        if name not in self.sems:
            self.sems[name] = self.nc.alloc_semaphore(name)
            self.cnt[name] = 0
        return self.sems[name]

    def wait(self, eng, rel):
        for sem, val in _evs(rel):
            if val is None or val <= 0:
                continue
            self.sem(sem)
            assert val <= self.cnt[sem], \
                f"wait {eng} {sem}>={val} exceeds planned {self.cnt[sem]}"
            key = (eng, sem)
            if self.lastw.get(key, 0) >= val:
                continue
            self.lastw[key] = val
            s = self.sems[sem]
            self.ops[eng].append(lambda e, s=s, v=val: e.wait_ge(s, v))

    def op(self, eng, meth, *args, inc=None, dma=False, **kw):
        n = 16 if dma else 1
        ev, s = None, None
        if inc is not None:
            self.sem(inc)
            self.cnt[inc] += n
            ev = (inc, self.cnt[inc])
            s = self.sems[inc]

        def f(e, meth=meth, args=args, kw=kw, s=s, n=n):
            r = getattr(e, meth)(*args, **kw)
            if s is not None:
                r.then_inc(s, n)
        self.ops[eng].append(f)
        return ev

    def build(self):
        with self.nc.Block() as block:
            for eng in self.ENGS:
                ops = self.ops[eng]
                getattr(block, eng)(lambda e, ops=ops: [f(e) for f in ops])


def build_nc(debug_taps=()):
    nc = bass.Bass("TRN2", target_bir_lowering=False)
    x_hbm = nc.dram_tensor("x_full", [S, DM], F32, kind="ExternalInput")
    xq_hbm = nc.dram_tensor("x_q", [TQ, DM], F32, kind="ExternalInput")
    fb_hbm = nc.dram_tensor("f32_blob", [F32OFF["_total"]], F32,
                            kind="ExternalInput")
    ub_hbm = nc.dram_tensor("u8_blob", [U8_TOTAL], U8, kind="ExternalInput")
    out_hbm = nc.dram_tensor("out", [TQ, DM], F32, kind="ExternalOutput")

    ident_hbm = nc.inline_tensor(np.eye(128, dtype=ml_dtypes.bfloat16), "identc")
    ones_hbm = nc.inline_tensor(np.ones((1, 128), dtype=np.float32), "onesc")

    def fpc(name, count):
        o = F32OFF[name]
        return fb_hbm[o:o + count]

    def row2d(ap, p):
        return ap.rearrange("(p n) -> p n", p=p)

    KB = 1024
    base = (nc.sbuf_base + 127) // 128 * 128

    def at(name, shape, dtype, off_kb):
        off = base + int(off_kb * KB)
        esz = {F32: 4, BF16: 2, U8: 1}[dtype]
        end = off + int(np.prod(shape[1:])) * esz
        assert end <= nc.sbuf_top, f"{name}: {end} > {nc.sbuf_top}"
        return nc.alloc_sbuf_tensor_at(name, shape, dtype, offset=off)

    IDENT = at("IDENT", [128, 128], BF16, 0)
    ONES = at("ONES", [1, 128], F32, 0.25)
    BQKV = at("BQKV", [128, 48], F32, 0.75)
    BOBC = at("BOBC", [128, 2048], F32, 1)
    DSBC = at("DSBC", [128, 2048], F32, 9)
    MASKS = at("MASKS", [128, 2048], BF16, 17)
    GSC = at("GSC", [128, 64], F32, 21)
    USC = at("USC", [128, 64], F32, 21.25)
    STAT = at("STAT", [128, 128], F32, 21.5)
    QBSB = at("QBSB", [128, 16], F32, 206.5)
    XHF = at("XHF", [128, NCH * 2048], BF16, 22)
    YS = at("YS", [128, QT * 2048], F32, 22)
    XQF = at("XQF", [128, NCH * 1024], BF16, 86)
    SIG = at("SIG", [128, 1024], F32, 86)
    HBUF = at("HBUF", [128, IGR * 1024], BF16, 90)
    GUD8 = [at(f"GUD8{i}", [128, 3 * 2048], U8, 106 + 6 * i) for i in range(2)]
    OS = at("OS", [128, HEADS * 1024], BF16, 118)
    YF = at("YF", [128, NCH * 1024], BF16, 118)
    BOROW = at("BOROW", [1, 2048], F32, 150)
    DSROW = at("DSROW", [1, 2048], F32, 158)
    XT = [at(f"XT{i}", [128, 2048], F32, 150 + 8 * i) for i in range(2)]
    XTM = [at(f"XTM{i}", [128, 2048], BF16, 166 + 4 * i) for i in range(2)]
    WQKVB = [at("WQKVB0", [128, 6144], BF16, 178.5),
             at("WQKVB1", [128, 6144], BF16, 150)]
    KS = [at("KS0", [128, 2048], BF16, 190.5), at("KS1", [128, 2048], BF16, 162)]
    VS = [at("VS0", [128, 2048], BF16, 194.5), at("VS1", [128, 2048], BF16, 166)]
    QH = [at("QH0", [128, 1024], BF16, 174.5),
          at("QH1", [128, 1024], BF16, 176.5)]
    VF = [at(f"VF{i}", [128, 512], BF16, 198.5 + 1.0 * i) for i in range(2)]
    PT = [at("PT", [128, 2048], BF16, 170),
          at("PTB", [128, 2048], BF16, 200.5)]
    PS = at("PS", [128, 1024], BF16, 204.5)
    WOC = [at("WOC0", [128, 4096], BF16, 150),
           at("WOC1", [128, 4096], BF16, 170)]
    YTM = [at(f"YTM{i}", [128, 2048], BF16, 162 + 4 * i) for i in range(2)]
    GD = at("GD", [128, 2048], BF16, 150)
    UD = at("UD", [128, 2048], BF16, 154)
    DDG = at("DDG", [128, IGR * 2048], BF16, 158)

    SC = nc.alloc_psum_tensor("SC", [128, 2048], F32)
    psum_mark = nc.psum_base
    OP = nc.alloc_psum_tensor("OP", [128, 128], F32)
    TP = [nc.alloc_psum_tensor(f"TP{i}", [128, 1024], BF16)
          for i in range(2)]
    PJP = nc.alloc_psum_tensor("PJP", [128, 512], F32)

    p = Plan(nc)
    ST, SV, SS, SX, SW, SO = "st", "sv", "ss", "sx", "sw", "so"
    taps = []
    EPSC = STAT[:, 127:128]
    p.op("vector", "memset", EPSC, EPS)
    QBS = QBSB[:, :]

    tap_evs = []

    def tap(name, buf, shape, dtype, wait_evs):
        if name not in debug_taps:
            return
        d = nc.dram_tensor(f"dbg_{name}", shape, dtype, kind="ExternalOutput")
        p.wait("gpsimd", wait_evs)
        if tap_evs:
            p.wait("gpsimd", tap_evs[-1])
        tap_evs.append(p.op("gpsimd", "dma_start", out=d[:, :], in_=buf,
                            inc=SO, dma=True))
        taps.append(f"dbg_{name}")

    ev_c0 = None
    for dst, src in ((IDENT, ident_hbm), (ONES, ones_hbm)):
        p.wait("gpsimd", ev_c0)
        ev_c0 = p.op("gpsimd", "dma_start", out=dst[:, :], in_=src[:, :],
                     inc="swc", dma=True)
    for dst, nm, np_ in ((BQKV, "bqkv", 128), (BOROW, "bo_row", 1),
                         (GSC, "gs", 128), (USC, "us", 128),
                         (DSROW, "ds_row", 1), (MASKS, "masks", 128)):
        cnt = int(np.prod(dst.shape))
        p.wait("gpsimd", ev_c0)
        ev_c0 = p.op("gpsimd", "dma_start", out=dst[:, :],
                     in_=row2d(fpc(nm, cnt), np_), inc="swc", dma=True)
    consts_ev = ev_c0

    p.wait("tensor", consts_ev)
    bcast_st = None
    sc_free = None
    for src, dst in ((BOROW, BOBC), (DSROW, DSBC)):
        p.wait("tensor", sc_free)
        for d in range(4):
            bcast_st = p.op("tensor", "matmul", SC[:, d * 512:(d + 1) * 512],
                            ONES[:, :], src[:, d * 512:(d + 1) * 512],
                            start=True, stop=True,
                            inc=ST if d == 3 else None)
        p.wait("vector", bcast_st)
        sc_free = p.op("vector", "tensor_copy", dst[:, :], SC[:, :], inc=SV)
    p.wait("vector", consts_ev)
    qbs_ev = p.op("vector", "tensor_scalar_mul", QBS,
                  BQKV[:, 0:48].rearrange("p (h t) -> p h t", t=3)[:, :, 0],
                  RS128, inc=SV)

    def ln_stats(src_ap, statbase, ready_ev):
        bst = STAT[:, statbase:statbase + 24]
        mv = STAT[:, statbase + 24:statbase + 26]
        sd = STAT[:, statbase + 26:statbase + 27]
        rstd = STAT[:, statbase + 27:statbase + 28]
        p.wait("vector", ready_ev)
        ev_b = None
        for q in range(4):
            ev_b = p.op("vector", "bn_stats", bst[:, 6 * q:6 * q + 6],
                        src_ap[:, q * 512:(q + 1) * 512],
                        inc=SV if q == 3 else None)
        p.wait("vector", ev_b)
        ev_a = p.op("vector", "bn_aggr", mv, bst, inc=SV)
        p.wait("scalar", ev_a)
        ev_sd = p.op("scalar", "activation", sd, mv[:, 1:2], AF.Sqrt,
                     bias=EPSC, inc=SS)
        p.wait("vector", ev_sd)
        ev_ln = p.op("vector", "reciprocal", rstd, sd, inc=SV)
        return mv[:, 0:1], rstd, ev_ln

    tp_rel = [None, None]
    tp_ctr = [0]

    def transpose16(src_tile, dst, dst_col, dst_stride, norm_ev, tp_bufs,
                    tp_rel_l, copies="alt"):
        ev_t = None
        ev_c = []
        dst3 = dst[:, :].rearrange("p (c n) -> p c n", n=dst_stride)
        for g in range(4):
            b = tp_ctr[0] % 2
            tp_ctr[0] += 1
            p.wait("tensor", [norm_ev, tp_rel_l[b]])
            for cc in range(4):
                c = 4 * g + cc
                ev_t = p.op("tensor", "transpose",
                            tp_bufs[b][:, cc * 128:(cc + 1) * 128],
                            src_tile[:, c * 128:(c + 1) * 128], IDENT[:, :],
                            inc=ST if cc == 3 else None)
            ceng = "scalar" if copies == "scalar" or b == 1 else "vector"
            p.wait(ceng, ev_t)
            ev = p.op(ceng, "copy" if ceng == "scalar" else "tensor_copy",
                      dst3[:, 4 * g:4 * g + 4, dst_col:dst_col + 128],
                      tp_bufs[b][:, 0:512].rearrange("p (c n) -> p c n", n=128),
                      inc=SS if ceng == "scalar" else SV)
            tp_rel_l[b] = ev
            if g >= 2:
                ev_c.append(ev)
        return ev_t, ev_c

    ev_wqkv = {}
    ks_ready = {}
    vs_ready = {}
    q_ready = {}
    ks_rel = [None, None]
    vs_rel = [None, None]
    qh_rel = [None, None]
    vf_rel = [None, None]
    pjp_rel = None
    wqkv_extra = [[], None]
    xhf_tile_ev = {}
    xqf_done = None
    proj_last_mm = [None, None]

    def emit_wqkv_dma(hh):
        par = hh % 2
        p.wait("gpsimd", [proj_last_mm[par], wqkv_extra[par]])
        ev_wqkv[hh] = p.op("gpsimd", "dma_start", out=WQKVB[par][:, :],
                           in_=row2d(fpc(f"wqkv{hh}", 128 * 6144), 128),
                           inc=f"sw{par}", dma=True)

    def emit_proj_unit(hh, u):
        nonlocal vf_rel, pjp_rel
        par = hh % 2
        if u < 4:
            sg = u
            p.wait("tensor", [ev_wqkv[hh], pjp_rel,
                              xhf_tile_ev.get(4 * sg + 3)])
            ev_mm = None
            for c in range(NCH):
                ev_mm = p.op("tensor", "matmul", PJP[:, :],
                             WQKVB[par][:, 2048 + c * 128: 2048 + (c + 1) * 128],
                             XHF[:, c * 2048 + sg * 512: c * 2048 + sg * 512 + 512],
                             start=(c == 0), stop=(c == NCH - 1),
                             inc=ST if c == NCH - 1 else None)
            proj_last_mm[par] = ev_mm
            p.wait("scalar", [ev_mm, ks_rel[par]])
            ev = p.op("scalar", "activation",
                      KS[par][:, sg * 512:(sg + 1) * 512], PJP[:, :],
                      AF.Identity, bias=BQKV[:, 3 * hh + 1:3 * hh + 2], inc=SS)
            pjp_rel = ev
            if sg == 3:
                ks_ready[hh] = ev
        elif u < 8:
            sg = u - 4
            vslot = sg % 2
            p.wait("tensor", [ev_wqkv[hh], pjp_rel,
                              xhf_tile_ev.get(4 * sg + 3)])
            ev_mm = None
            for c in range(NCH):
                ev_mm = p.op("tensor", "matmul", PJP[:, :],
                             WQKVB[par][:, 4096 + c * 128: 4096 + (c + 1) * 128],
                             XHF[:, c * 2048 + sg * 512: c * 2048 + sg * 512 + 512],
                             start=(c == 0), stop=(c == NCH - 1),
                             inc=ST if c == NCH - 1 else None)
            proj_last_mm[par] = ev_mm
            p.wait("scalar", [ev_mm, vf_rel[vslot]])
            ev = p.op("scalar", "activation",
                      VF[vslot][:, :], PJP[:, :],
                      AF.Identity, bias=BQKV[:, 3 * hh + 2:3 * hh + 3], inc=SS)
            pjp_rel = ev
            b = tp_ctr[0] % 2
            tp_ctr[0] += 1
            p.wait("tensor", [ev, tp_rel[b]])
            ev_vt = None
            for q4 in range(4):
                ev_vt = p.op("tensor", "transpose",
                             TP[b][:, q4 * 128:(q4 + 1) * 128],
                             VF[vslot][:, q4 * 128:(q4 + 1) * 128],
                             IDENT[:, :], inc=ST if q4 == 3 else None)
            ceng = "vector" if b == 0 else "scalar"
            p.wait(ceng, [ev_vt, vs_rel[par]])
            ev_c = p.op(ceng, "tensor_copy" if b == 0 else "copy",
                        VS[par][:, sg * 512:(sg + 1) * 512],
                        TP[b][:, 0:512], inc=SV if b == 0 else SS)
            tp_rel[b] = ev_c
            vf_rel[vslot] = ev_vt
            if sg == 3:
                vs_ready[hh] = ev_c
        else:
            tg = u - 8
            p.wait("tensor", [ev_wqkv[hh], pjp_rel, xqf_done])
            ev_mm = None
            for c in range(NCH):
                ev_mm = p.op("tensor", "matmul", PJP[:, :],
                             WQKVB[par][:, c * 128:(c + 1) * 128],
                             XQF[:, c * 1024 + tg * 512: c * 1024 + tg * 512 + 512],
                             start=(c == 0), stop=(c == NCH - 1),
                             inc=ST if c == NCH - 1 else None)
            proj_last_mm[par] = ev_mm
            p.wait("scalar", [ev_mm, qh_rel[par], qbs_ev])
            ev = p.op("scalar", "activation",
                      QH[par][:, tg * 512:(tg + 1) * 512], PJP[:, :],
                      AF.Identity, bias=QBS[:, hh:hh + 1], scale=RS128,
                      inc=SS)
            pjp_rel = ev
            if tg == 1:
                q_ready[hh] = ev

    emit_wqkv_dma(0)
    wo_bf = nc.dram_tensor("wo_bf", [2 * NCH * 128 * 1024], BF16,
                           kind="Internal")
    ev_wobf = p.op("gpsimd", "dma_start",
                   out=wo_bf[:].rearrange("(p n) -> p n", p=2048),
                   in_=fpc("wo0_0", 2 * NCH * 128 * 1024).rearrange(
                       "(p n) -> p n", p=2048),
                   inc="swo", dma=True)

    xt_rel = [bcast_st, bcast_st]
    xtm_rel = [None, None]
    xt_readers = [None, None]
    p.wait("tensor", consts_ev)
    weave = {3: [0], 5: [4], 7: [1], 9: [5], 11: [2], 13: [6], 15: [3, 7]}
    tiles = [("f", i) for i in range(NCH)] + [("q", i) for i in range(QT)]
    pend = None

    def do_transposes(job):
        nonlocal xqf_done
        kind, i, par, ev_norm = job
        if kind == "q":
            ev_t, ev_c = transpose16(XTM[par], XQF, i * 128, 1024, ev_norm,
                                     TP, tp_rel, copies="scalar")
            xqf_done = ev_c
        else:
            ev_t, ev_c = transpose16(XTM[par], XHF, i * 128, 2048, ev_norm,
                                     TP, tp_rel, copies="scalar")
            xhf_tile_ev[i] = ev_c
            for u in weave.get(i, []):
                emit_proj_unit(0, u)
        xtm_rel[par] = ev_t

    for k, (kind, i) in enumerate(tiles):
        par = k % 2
        p.wait("sync", xt_rel[par])
        src_hbm = xq_hbm if kind == "q" else x_hbm
        ev_x = p.op("sync", "dma_start", out=XT[par][:, :],
                    in_=src_hbm[i * 128:(i + 1) * 128, :], inc=f"sx{par}",
                    dma=True)
        mean, rstd, ev_ln = ln_stats(XT[par][:, :], par * 32, ev_x)
        p.wait("vector", [ev_ln, xtm_rel[par]])
        ev_norm = p.op("vector", "tensor_scalar", XTM[par][:, :], XT[par][:, :],
                       mean, rstd, op0=ALU.subtract, op1=ALU.mult, inc=SV)
        xt_rel[par] = [ev_norm]
        xt_readers[par] = [ev_norm]
        if pend is not None:
            do_transposes(pend)
        pend = (kind, i, par, ev_norm)
    do_transposes(pend)
    for u in (8, 9):
        emit_proj_unit(0, u)
    x_done = xt_readers[0] + xt_readers[1] + xtm_rel
    xhf_done = xhf_tile_ev[NCH - 1]
    wqkv_extra[1] = x_done

    tap("XHF", XHF[:, :], [128, NCH * 2048], BF16, [xhf_done])
    tap("XQF", XQF[:, :], [128, NCH * 1024], BF16, [xqf_done])

    sc_rel = None
    sc_half_rel = [None, None]
    os_rel = None
    ps_rel = [None, None]
    pt_rel = [None, None]
    attn_last = None
    pend_tile = [None]

    def emit_tav():
        nonlocal attn_last, os_rel
        if pend_tile[0] is None:
            return
        h, par, j, nb, pbuf, ev_pn = pend_tile[0]
        pend_tile[0] = None
        ev_av = None
        ev_tp_last = None
        pend_av = []

        def emit_av():
            nonlocal ev_av
            g0, gn, bk, ev_psc = pend_av.pop(0)
            p.wait("tensor", ev_psc)
            if g0 == 0:
                p.wait("tensor", [os_rel, vs_ready[h]])
            for cc in range(gn):
                b = g0 + cc
                ev_av = p.op("tensor", "matmul", OP[:, :],
                             VS[par][:, b * 128:(b + 1) * 128],
                             PS[:, bk * 512 + cc * 128:
                                 bk * 512 + (cc + 1) * 128],
                             start=(b == 0), stop=(b == nb - 1),
                             inc=ST if cc == gn - 1 else None)
            ps_rel[bk] = ev_av

        for g0 in range(0, nb, 4):
            gn = min(4, nb - g0)
            bk = tp_ctr[0] % 2
            tp_ctr[0] += 1
            p.wait("tensor", [ev_pn, tp_rel_tp[bk]])
            ev_tp = None
            for cc in range(gn):
                b = g0 + cc
                ev_tp = p.op("tensor", "transpose",
                             TP[bk][:, cc * 128:(cc + 1) * 128],
                             PT[pbuf][:, b * 128:(b + 1) * 128], IDENT[:, :],
                             inc=ST if cc == gn - 1 else None)
            ev_tp_last = ev_tp
            if g0 >= 4:
                emit_av()
            ceng = "vector" if bk == 0 else "scalar"
            p.wait(ceng, [ev_tp, ps_rel[bk]])
            ev_psc = p.op(ceng, "tensor_copy" if bk == 0 else "copy",
                          PS[:, bk * 512:bk * 512 + gn * 128],
                          TP[bk][:, 0:gn * 128], inc=SV if bk == 0 else SS)
            tp_rel_tp[bk] = ev_psc
            pend_av.append((g0, gn, bk, ev_psc))
        emit_av()
        pt_rel[pbuf] = ev_tp_last
        vs_rel[par] = ev_av
        attn_last = ev_av
        p.wait("scalar", ev_av)
        os_rel = p.op("scalar", "activation",
                      OS[:, h * 1024 + j * 128: h * 1024 + (j + 1) * 128],
                      OP[:, :], AF.Copy, inc=SS)

    tp_rel_tp = tp_rel
    for h in range(HEADS):
        par = h % 2
        if h + 1 < HEADS:
            emit_wqkv_dma(h + 1)
        nq = 10 if h + 1 < HEADS else 0
        for j in range(QT):
            ext = 256 * (j + 1)
            nb = 2 * (j + 1)
            sl = 112 + 4 * (j % 2)
            mneg = STAT[:, sl:sl + 1]
            ssum = STAT[:, sl + 1:sl + 2]
            rinv = STAT[:, sl + 2:sl + 3]
            if j <= 3:
                off = (j % 2) * 1024
                p.wait("tensor", [sc_half_rel[j % 2], ks_ready[h],
                                  q_ready[h]])
            else:
                off = 0
                p.wait("tensor", [sc_half_rel[0], sc_half_rel[1], sc_rel,
                                  ks_ready[h], q_ready[h]])
            SCV = SC[:, off:off + ext]
            ev_qk = None
            for seg in range(0, ext, 512):
                seg_w = min(512, ext - seg)
                ev_qk = p.op("tensor", "matmul", SCV[:, seg:seg + seg_w],
                             QH[par][:, j * 128:(j + 1) * 128],
                             KS[par][:, seg:seg + seg_w],
                             start=True, stop=True,
                             inc=ST if seg + seg_w >= ext else None)
            ks_rel[par] = ev_qk
            qh_rel[par] = ev_qk
            p.wait("vector", ev_qk)
            ev_msk = p.op("vector", "tensor_tensor", SCV[:, ext - 256:ext],
                          SCV[:, ext - 256:ext], MASKS[:, j * 256:(j + 1) * 256],
                          ALU.add, inc=SV)
            p.wait("vector", ev_msk)
            ev_mx = p.op("vector", "reduce_max", mneg, SCV[:, 0:ext], axis=AX,
                         negate=True, inc=SV)
            pbuf = (h * QT + j) % 2
            p.wait("scalar", [ev_mx, pt_rel[pbuf]])
            ev_exp = p.op("scalar", "activation", PT[pbuf][:, 0:ext],
                          SCV[:, 0:ext],
                          AF.Exp, bias=mneg, scale=1.0, accum_out=ssum,
                          inc=SS)
            if j <= 3:
                sc_half_rel[j % 2] = ev_exp
            else:
                sc_rel = ev_exp
            p.wait("vector", ev_exp)
            ev_ri = p.op("vector", "reciprocal", rinv, ssum, inc=SV)
            p.wait("vector", ev_ri)
            ev_pn = p.op("vector", "tensor_scalar_mul", PT[pbuf][:, 0:ext],
                         PT[pbuf][:, 0:ext], rinv, inc=SV)
            lo = (nq * j) // QT
            hi = (nq * (j + 1)) // QT
            for u in range(lo, hi):
                emit_proj_unit(h + 1, u)
            emit_tav()
            pend_tile[0] = (h, par, j, nb, pbuf, ev_pn)
    emit_tav()
    os_done = os_rel
    xhf_rel = [proj_last_mm[0], proj_last_mm[1]]

    tap("OS", OS[:, :], [128, HEADS * 1024], BF16, [os_done])

    nc.psum_base = psum_mark
    TP2 = [nc.alloc_psum_tensor(f"TP2_{i}", [128, 1024], BF16)
           for i in range(2)]
    tp_rel2 = [[os_done] + tp_rel + [pjp_rel], [os_done] + tp_rel + [pjp_rel]]

    ev_xq = []
    p.wait("gpsimd", xhf_rel)
    p.wait("sync", list(tap_evs))
    p.wait("gpsimd", list(tap_evs))
    for j in range(QT):
        if j:
            p.wait("gpsimd", ev_xq[-1])
        ev_xq.append(p.op("gpsimd", "dma_start",
                          out=YS[:, j * 2048:(j + 1) * 2048],
                          in_=xq_hbm[j * 128:(j + 1) * 128, :], inc="sxq",
                          dma=True))

    pre_u8 = [None, None]
    for par8 in range(2):
        if par8 == 0:
            p.wait("scalar", xhf_rel)
        pre_u8[par8] = p.op(
            "scalar", "dma_start", out=GUD8[par8][:, :],
            in_=row2d(ub_hbm[par8 * U8_CHUNK:(par8 + 1) * U8_CHUNK], 128),
            inc=f"su{par8}", dma=True)

    woc_rel = [[attn_last], [attn_last]]
    pj_rel = [[sc_rel, sc_half_rel[0], sc_half_rel[1]],
              [sc_rel, sc_half_rel[0], sc_half_rel[1]]]
    ev_y = {}
    ytm_rel = [None, None]
    yf_done = None
    for jq in range(4):
        for dh in range(2):
            ev_wo_j = [None, None]
            for c in range(NCH):
                cb = (c // 4) % 2
                if c % 4 == 0:
                    qeng = "sync" if cb == 0 else "gpsimd"
                    p.wait(qeng, [woc_rel[cb], ev_wobf])
                    if cb and ev_xq:
                        p.wait(qeng, ev_xq[-1])
                    ev_woc = p.op(
                        qeng, "dma_start",
                        out=WOC[cb][:, :].rearrange("p (c n) -> p c n", c=4),
                        in_=wo_bf[((dh * NCH + c) * 128) * 1024:
                                  ((dh * NCH + c + 4) * 128) * 1024
                                  ].rearrange("(c p n) -> p c n", c=4, p=128),
                        inc="sx0" if cb == 0 else "swoc", dma=True)
                p.wait("tensor", ev_woc)
                if c == 0:
                    p.wait("tensor", [os_done, pj_rel[0], pj_rel[1]])
                ev_mm = None
                for jl in range(2):
                    j = 2 * jq + jl
                    PJ = SC[:, jl * 1024:(jl + 1) * 1024]
                    for d in range(2):
                        last_c = (c == NCH - 1)
                        do_inc = (last_c and d == 1) or (jl == 1 and d == 1)
                        ev_mm = p.op(
                            "tensor", "matmul", PJ[:, d * 512:(d + 1) * 512],
                            OS[:, c * 1024 + j * 128:
                                c * 1024 + (j + 1) * 128],
                            WOC[cb][:, (c % 4) * 1024 + d * 512:
                                (c % 4) * 1024 + (d + 1) * 512],
                            start=(c == 0), stop=last_c,
                            inc=ST if do_inc else None)
                        if last_c and d == 1:
                            ev_wo_j[jl] = ev_mm
                woc_rel[cb] = ev_mm
            for jl in range(2):
                j = 2 * jq + jl
                PJ = SC[:, jl * 1024:(jl + 1) * 1024]
                ysl = YS[:, j * 2048 + dh * 1024: j * 2048 + dh * 1024 + 1024]
                p.wait("vector", [ev_wo_j[jl], ev_xq[j]])
                ev0 = p.op("vector", "scalar_tensor_tensor", ysl, PJ, 0.0, ysl,
                           op0=ALU.add, op1=ALU.add, inc=SV)
                p.wait("vector", ev0)
                ev = p.op("vector", "tensor_tensor", ysl, ysl,
                          BOBC[:, dh * 1024: dh * 1024 + 1024], ALU.add,
                          inc=SV)
                pj_rel[jl] = ev
                ev_y[(j, dh)] = ev
        for jl in range(2):
            j = 2 * jq + jl
            par = j % 2
            ytile = YS[:, j * 2048:(j + 1) * 2048]
            ready = [ev_y[(j, 0)], ev_y[(j, 1)]]
            mean, rstd, ev_ln = ln_stats(ytile, 56 + par * 28, ready)
            p.wait("vector", [ev_ln, ytm_rel[par]])
            ev_norm = p.op("vector", "tensor_scalar", YTM[par][:, :], ytile,
                           mean, rstd, op0=ALU.subtract, op1=ALU.mult, inc=SV)
            ev_t, ev_c = transpose16(YTM[par], YF, j * 128, 1024, ev_norm,
                                     TP2, tp_rel2)
            ytm_rel[par] = ev_t
            yf_done = ev_c
    wo_done_t = [woc_rel[0], woc_rel[1]]

    tap("YS_W", YS[:, :], [128, QT * 2048], F32,
        [ev_y[(j, dh)] for j in range(QT) for dh in range(2)])
    tap("YF", YF[:, :], [128, NCH * 1024], BF16, [yf_done])

    nc.psum_base = psum_mark
    PB2 = nc.alloc_psum_tensor("PB2", [128, 2048], F32)
    GP = PB2[:, 0:1024]
    UP = PB2[:, 1024:2048]

    gud_rel = [None, None]
    gd_rel = ud_rel = None
    ddg_rel = None
    sig_rel = None
    hbuf_rel = [None] * IGR
    up_rel = None
    dpsum_rel = {0: [pj_rel[0], pj_rel[1]],
                 1: [yf_done, tp_rel2[0], tp_rel2[1]]}
    ev_h_grp = None
    final_evac = {}

    p.wait("sync", list(tap_evs))
    p.wait("scalar", list(tap_evs))
    p.wait("vector", list(tap_evs))
    for g in range(IGR):
        ev_dd = None
        for cc in range(IGR):
            ic = g * IGR + cc
            par = ic % 2
            if ic < 2:
                ev_u8 = pre_u8[ic]
            else:
                p.wait("sync", gud_rel[par])
                ev_u8 = p.op("sync", "dma_start", out=GUD8[par][:, :],
                             in_=row2d(ub_hbm[ic * U8_CHUNK:
                                              (ic + 1) * U8_CHUNK], 128),
                             inc=f"su{par}", dma=True)
            p.wait("scalar", [ev_u8, gd_rel])
            if ic == 0:
                p.wait("scalar", wo_done_t)
                p.wait("vector", wo_done_t)
            ev_gd = p.op("scalar", "activation", GD[:, :],
                         GUD8[par][:, 0:2048], AF.Copy, bias=-128.0, inc=SS)
            p.wait("scalar", ud_rel)
            ev_ud = p.op("scalar", "activation", UD[:, :],
                         GUD8[par][:, 2048:4096], AF.Copy, bias=-128.0, inc=SS)
            p.wait("vector", [ev_u8, ddg_rel])
            ev_dd = p.op("vector", "scalar_tensor_tensor",
                         DDG[:, cc * 2048:(cc + 1) * 2048],
                         GUD8[par][:, 4096:6144], -128.0, DSBC[:, :],
                         op0=ALU.add, op1=ALU.mult, inc=SV)
            gud_rel[par] = [ev_ud, ev_dd]
            p.wait("tensor", [ev_gd, sig_rel, yf_done, dpsum_rel[1]])
            ev_gm = None
            for c in range(NCH):
                for tg in range(2):
                    last = (c == NCH - 1 and tg == 1)
                    ev_gm = p.op("tensor", "matmul",
                                 GP[:, tg * 512:(tg + 1) * 512],
                                 GD[:, c * 128:(c + 1) * 128],
                                 YF[:, c * 1024 + tg * 512:
                                     c * 1024 + tg * 512 + 512],
                                 start=(c == 0), stop=(c == NCH - 1),
                                 inc=ST if last else None)
            gd_rel = ev_gm
            p.wait("tensor", [ev_ud, up_rel])
            ev_um = None
            for c in range(NCH):
                for tg in range(2):
                    last = (c == NCH - 1 and tg == 1)
                    ev_um = p.op("tensor", "matmul",
                                 UP[:, tg * 512:(tg + 1) * 512],
                                 UD[:, c * 128:(c + 1) * 128],
                                 YF[:, c * 1024 + tg * 512:
                                     c * 1024 + tg * 512 + 512],
                                 start=(c == 0), stop=(c == NCH - 1),
                                 inc=ST if last else None)
            ud_rel = ev_um
            p.wait("scalar", [ev_gm, up_rel])
            sig_rel = p.op("scalar", "activation", SIG[:, :], GP[:, :],
                           AF.Silu, scale=GSC[:, ic:ic + 1], inc=SS)
            p.wait("vector", [sig_rel, ev_um, hbuf_rel[cc]])
            ev_h = p.op("vector", "scalar_tensor_tensor",
                        HBUF[:, cc * 1024:(cc + 1) * 1024], UP[:, :],
                        USC[:, ic:ic + 1], SIG[:, :],
                        op0=ALU.mult, op1=ALU.mult, inc=SV)
            up_rel = ev_h
            if cc == IGR - 1:
                ev_h_grp = ev_h
        if g == 0:
            tap("HBUF0", HBUF[:, :], [128, IGR * 1024], BF16, [ev_h_grp])
            tap("DDG0", DDG[:, :], [128, IGR * 2048], BF16, [ev_dd])
            if debug_taps:
                p.wait("sync", list(tap_evs))
                p.wait("scalar", list(tap_evs))
                p.wait("vector", list(tap_evs))
                p.wait("gpsimd", list(tap_evs))
        ev_dn_last = None
        for j in range(QT):
            DP = SC if j % 2 == 0 else PB2
            p.wait("tensor", [ev_h_grp, ev_dd, dpsum_rel[j % 2]])
            ev_mm = None
            for cc2 in range(IGR):
                for d in range(4):
                    last = (cc2 == IGR - 1 and d == 3)
                    ev_mm = p.op(
                        "tensor", "matmul", DP[:, d * 512:(d + 1) * 512],
                        HBUF[:, cc2 * 1024 + j * 128:
                             cc2 * 1024 + (j + 1) * 128],
                        DDG[:, cc2 * 2048 + d * 512:
                            cc2 * 2048 + d * 512 + 512],
                        start=(cc2 == 0), stop=(cc2 == IGR - 1),
                        inc=ST if last else None)
            ev_dn_last = ev_mm
            p.wait("vector", [ev_mm, dpsum_rel[j % 2],
                              ev_y[(j, 0)], ev_y[(j, 1)]])
            ev_ev = p.op("vector", "tensor_tensor",
                         YS[:, j * 2048:(j + 1) * 2048],
                         YS[:, j * 2048:(j + 1) * 2048], DP[:, :], ALU.add,
                         inc=SV)
            dpsum_rel[j % 2] = ev_ev
            if g == IGR - 1:
                final_evac[j] = ev_ev
        ddg_rel = ev_dn_last
        for cc2 in range(IGR):
            hbuf_rel[cc2] = ev_dn_last

    ev_out = [None, None]
    for j in range(QT):
        qeng = "gpsimd" if j % 2 == 0 else "scalar"
        p.wait(qeng, final_evac[j])
        ev_out[j % 2] = p.op(qeng, "dma_start",
                             out=out_hbm[j * 128:(j + 1) * 128, :],
                             in_=YS[:, j * 2048:(j + 1) * 2048],
                             inc=SO if j % 2 == 0 else "so2", dma=True)
    p.wait("gpsimd", ev_out[0])
    p.wait("gpsimd", ev_out[1])

    p.build()
    return nc, taps


def kernel(**inputs):
    cores = pack_host(inputs)
    nc, _ = build_nc()
    in_maps = [dict(x_full=c["x_full"], x_q=c["x_q"], f32_blob=c["f32_blob"],
                    u8_blob=c["u8_blob"]) for c in cores]
    res = run_bass_kernel_spmd(nc, in_maps, core_ids=list(range(8)))
    out = np.empty((4, 2048, 2048), dtype=np.float32)
    for c in range(8):
        out[c // 2][_q_rows(c % 2)] = res.results[c]["out"]
    return out


# version 25
# speedup vs baseline: 1.0310x; 1.0013x over previous
import math
import numpy as np
import ml_dtypes
import concourse.bass as bass
import concourse.mybir as mybir
from concourse.bass_utils import run_bass_kernel_spmd

F32 = mybir.dt.float32
BF16 = mybir.dt.bfloat16
U8 = mybir.dt.uint8
AF = mybir.ActivationFunctionType
ALU = mybir.AluOpType
AX = mybir.AxisListType.X

DM = 2048
NCH = 16
S = 2048
TQ = 1024
QT = 8
HEADS = 16
DH = 128
ICH = 64
IGR = 8
RS128 = 1.0 / math.sqrt(128.0)
EPS = 1e-5


def _f32_layout():
    off, d = 0, {}

    def add(name, elems):
        nonlocal off
        d[name] = off
        off += elems

    add("bqkv", 128 * 48)
    add("bo_row", 2048)
    add("gs", 128 * 64)
    add("us", 128 * 64)
    add("ds_row", 2048)
    add("masks", 128 * 2048)
    for h in range(HEADS):
        add(f"wqkv{h}", 128 * 6144)
    for dh in range(2):
        for c in range(NCH):
            add(f"wo{dh}_{c}", 128 * 1024)
    d["_total"] = off
    return d


F32OFF = _f32_layout()
U8_CHUNK = 3 * 128 * 2048
U8_TOTAL = ICH * U8_CHUNK


def _q_rows(w):
    tiles = np.arange(QT) * 2 + w
    return (tiles[:, None] * 128 + np.arange(128)[None, :]).reshape(-1)


def _build_masks(w):
    m = np.zeros((128, QT, 256), dtype=np.float32)
    r = np.arange(128)[:, None]
    i = np.arange(128)[None, :]
    diag = np.where(i > r, np.float32(-10000.0), np.float32(0.0))
    for j in range(QT):
        if w == 0:
            m[:, j, 0:128] = diag
            m[:, j, 128:256] = -10000.0
        else:
            m[:, j, 128:256] = diag
    return m.reshape(128, QT * 256)


def pack_host(inputs):
    x = np.ascontiguousarray(np.asarray(inputs["L0_attn_x"], dtype=np.float32))
    wqkv = np.ascontiguousarray(np.asarray(inputs["L0_attn_Wqkv"], dtype=np.float32))
    bqkv = np.asarray(inputs["L0_attn_bqkv"], dtype=np.float32)
    wo = np.ascontiguousarray(np.asarray(inputs["L0_attn_Wo"], dtype=np.float32))
    bo = np.asarray(inputs["L0_attn_bo"], dtype=np.float32)
    gw = np.ascontiguousarray(np.asarray(inputs["L0_ffn_gate_w"], dtype=np.int32))
    gsc = np.asarray(inputs["L0_ffn_gate_scale"], dtype=np.float32)
    uw = np.ascontiguousarray(np.asarray(inputs["L0_ffn_up_w"], dtype=np.int32))
    usc = np.asarray(inputs["L0_ffn_up_scale"], dtype=np.float32)
    dw = np.ascontiguousarray(np.asarray(inputs["L0_ffn_down_w"], dtype=np.int32))
    dsc = np.asarray(inputs["L0_ffn_down_scale"], dtype=np.float32)

    blob = np.empty(F32OFF["_total"], dtype=np.float32)

    def put(name, arr):
        a = np.ascontiguousarray(arr, dtype=np.float32).reshape(-1)
        blob[F32OFF[name]:F32OFF[name] + a.size] = a

    put("bqkv", bqkv.reshape(3, 16, 128).transpose(2, 1, 0))
    put("bo_row", bo)
    put("gs", gsc.reshape(64, 128).T)
    put("us", usc.reshape(64, 128).T)
    put("ds_row", dsc)
    wq5 = wqkv.reshape(NCH, 128, 3, HEADS, DH)
    for h in range(HEADS):
        qkv = np.stack([wq5[:, :, t, h, :] for t in range(3)], axis=0)
        put(f"wqkv{h}", qkv.transpose(2, 0, 1, 3))
    wo4 = wo.reshape(NCH, 128, 2, 1024)
    for dh in range(2):
        for c in range(NCH):
            put(f"wo{dh}_{c}", wo4[c, :, dh, :])

    g8 = gw.view(np.uint8).reshape(8192, 2048, 4)[:, :, 0]
    u8 = uw.view(np.uint8).reshape(8192, 2048, 4)[:, :, 0]
    d8 = dw.view(np.uint8).reshape(2048, 8192, 4)[:, :, 0]
    gt = g8.reshape(ICH, 128, NCH, 128).transpose(0, 3, 2, 1).reshape(ICH, 128, 2048)
    ut = u8.reshape(ICH, 128, NCH, 128).transpose(0, 3, 2, 1).reshape(ICH, 128, 2048)
    dt = d8.reshape(2048, ICH, 128).transpose(1, 2, 0)
    ublob = np.ascontiguousarray(np.stack([gt, ut, dt], axis=2)).reshape(-1)

    fblobs = []
    for w in range(2):
        fb = blob.copy()
        fb[F32OFF["masks"]:F32OFF["masks"] + 128 * 2048] = \
            _build_masks(w).reshape(-1)
        fblobs.append(fb)

    cores = []
    for c in range(8):
        seq, w = c // 2, c % 2
        cores.append(dict(
            x_full=np.ascontiguousarray(x[seq]),
            x_q=np.ascontiguousarray(x[seq][_q_rows(w)]),
            f32_blob=fblobs[w],
            u8_blob=ublob,
        ))
    return cores


def _evs(rel):
    if rel is None:
        return []
    if isinstance(rel, tuple):
        return [rel]
    return [e for r in rel for e in _evs(r)]


class Plan:
    ENGS = ("sync", "gpsimd", "vector", "scalar", "tensor")

    def __init__(self, nc):
        self.nc = nc
        self.ops = {e: [] for e in self.ENGS}
        self.sems = {}
        self.cnt = {}
        self.lastw = {}

    def sem(self, name):
        if name not in self.sems:
            self.sems[name] = self.nc.alloc_semaphore(name)
            self.cnt[name] = 0
        return self.sems[name]

    def wait(self, eng, rel):
        for sem, val in _evs(rel):
            if val is None or val <= 0:
                continue
            self.sem(sem)
            assert val <= self.cnt[sem], \
                f"wait {eng} {sem}>={val} exceeds planned {self.cnt[sem]}"
            key = (eng, sem)
            if self.lastw.get(key, 0) >= val:
                continue
            self.lastw[key] = val
            s = self.sems[sem]
            self.ops[eng].append(lambda e, s=s, v=val: e.wait_ge(s, v))

    def op(self, eng, meth, *args, inc=None, dma=False, **kw):
        n = 16 if dma else 1
        ev, s = None, None
        if inc is not None:
            self.sem(inc)
            self.cnt[inc] += n
            ev = (inc, self.cnt[inc])
            s = self.sems[inc]

        def f(e, meth=meth, args=args, kw=kw, s=s, n=n):
            r = getattr(e, meth)(*args, **kw)
            if s is not None:
                r.then_inc(s, n)
        self.ops[eng].append(f)
        return ev

    def build(self):
        with self.nc.Block() as block:
            for eng in self.ENGS:
                ops = self.ops[eng]
                getattr(block, eng)(lambda e, ops=ops: [f(e) for f in ops])


def build_nc(debug_taps=()):
    nc = bass.Bass("TRN2", target_bir_lowering=False)
    x_hbm = nc.dram_tensor("x_full", [S, DM], F32, kind="ExternalInput")
    xq_hbm = nc.dram_tensor("x_q", [TQ, DM], F32, kind="ExternalInput")
    fb_hbm = nc.dram_tensor("f32_blob", [F32OFF["_total"]], F32,
                            kind="ExternalInput")
    ub_hbm = nc.dram_tensor("u8_blob", [U8_TOTAL], U8, kind="ExternalInput")
    out_hbm = nc.dram_tensor("out", [TQ, DM], F32, kind="ExternalOutput")

    ident_hbm = nc.inline_tensor(np.eye(128, dtype=ml_dtypes.bfloat16), "identc")
    ones_hbm = nc.inline_tensor(np.ones((1, 128), dtype=np.float32), "onesc")

    def fpc(name, count):
        o = F32OFF[name]
        return fb_hbm[o:o + count]

    def row2d(ap, p):
        return ap.rearrange("(p n) -> p n", p=p)

    KB = 1024
    base = (nc.sbuf_base + 127) // 128 * 128

    def at(name, shape, dtype, off_kb):
        off = base + int(off_kb * KB)
        esz = {F32: 4, BF16: 2, U8: 1}[dtype]
        end = off + int(np.prod(shape[1:])) * esz
        assert end <= nc.sbuf_top, f"{name}: {end} > {nc.sbuf_top}"
        return nc.alloc_sbuf_tensor_at(name, shape, dtype, offset=off)

    IDENT = at("IDENT", [128, 128], BF16, 0)
    ONES = at("ONES", [1, 128], F32, 0.25)
    BQKV = at("BQKV", [128, 48], F32, 0.75)
    BOBC = at("BOBC", [128, 2048], F32, 1)
    DSBC = at("DSBC", [128, 2048], F32, 9)
    MASKS = at("MASKS", [128, 2048], BF16, 17)
    GSC = at("GSC", [128, 64], F32, 21)
    USC = at("USC", [128, 64], F32, 21.25)
    STAT = at("STAT", [128, 128], F32, 21.5)
    QBSB = at("QBSB", [128, 16], F32, 206.5)
    XHF = at("XHF", [128, NCH * 2048], BF16, 22)
    YS = at("YS", [128, QT * 2048], F32, 22)
    XQF = at("XQF", [128, NCH * 1024], BF16, 86)
    SIG = at("SIG", [128, 1024], F32, 86)
    HBUF = at("HBUF", [128, IGR * 1024], BF16, 90)
    GUD8 = [at(f"GUD8{i}", [128, 3 * 2048], U8, 106 + 6 * i) for i in range(2)]
    OS = at("OS", [128, HEADS * 1024], BF16, 118)
    YF = at("YF", [128, NCH * 1024], BF16, 118)
    BOROW = at("BOROW", [1, 2048], F32, 150)
    DSROW = at("DSROW", [1, 2048], F32, 158)
    XT = [at(f"XT{i}", [128, 2048], F32, 150 + 8 * i) for i in range(2)]
    XTM = [at(f"XTM{i}", [128, 2048], BF16, 166 + 4 * i) for i in range(2)]
    WQKVB = [at("WQKVB0", [128, 6144], BF16, 178.5),
             at("WQKVB1", [128, 6144], BF16, 150)]
    KS = [at("KS0", [128, 2048], BF16, 190.5), at("KS1", [128, 2048], BF16, 162)]
    VS = [at("VS0", [128, 2048], BF16, 194.5), at("VS1", [128, 2048], BF16, 166)]
    QH = [at("QH0", [128, 1024], BF16, 174.5),
          at("QH1", [128, 1024], BF16, 176.5)]
    VF = [at(f"VF{i}", [128, 512], BF16, 198.5 + 1.0 * i) for i in range(2)]
    PT = [at("PT", [128, 2048], BF16, 170),
          at("PTB", [128, 2048], BF16, 200.5)]
    PS = at("PS", [128, 1024], BF16, 204.5)
    WOC = [at("WOC0", [128, 4096], BF16, 150),
           at("WOC1", [128, 4096], BF16, 170)]
    YTM = [at(f"YTM{i}", [128, 2048], BF16, 162 + 4 * i) for i in range(2)]
    GD = at("GD", [128, 2048], BF16, 150)
    UD = at("UD", [128, 2048], BF16, 154)
    DDG = at("DDG", [128, IGR * 2048], BF16, 158)

    SC = nc.alloc_psum_tensor("SC", [128, 2048], F32)
    psum_mark = nc.psum_base
    OP = nc.alloc_psum_tensor("OP", [128, 128], F32)
    TP = [nc.alloc_psum_tensor(f"TP{i}", [128, 1024], BF16)
          for i in range(2)]
    PJP = nc.alloc_psum_tensor("PJP", [128, 512], F32)

    p = Plan(nc)
    ST, SV, SS, SX, SW, SO = "st", "sv", "ss", "sx", "sw", "so"
    taps = []
    EPSC = STAT[:, 127:128]
    p.op("vector", "memset", EPSC, EPS)
    QBS = QBSB[:, :]

    tap_evs = []

    def tap(name, buf, shape, dtype, wait_evs):
        if name not in debug_taps:
            return
        d = nc.dram_tensor(f"dbg_{name}", shape, dtype, kind="ExternalOutput")
        p.wait("gpsimd", wait_evs)
        if tap_evs:
            p.wait("gpsimd", tap_evs[-1])
        tap_evs.append(p.op("gpsimd", "dma_start", out=d[:, :], in_=buf,
                            inc=SO, dma=True))
        taps.append(f"dbg_{name}")

    ev_c0 = None
    for dst, src in ((IDENT, ident_hbm), (ONES, ones_hbm)):
        p.wait("gpsimd", ev_c0)
        ev_c0 = p.op("gpsimd", "dma_start", out=dst[:, :], in_=src[:, :],
                     inc="swc", dma=True)
    for dst, nm, np_ in ((BQKV, "bqkv", 128), (BOROW, "bo_row", 1),
                         (GSC, "gs", 128), (USC, "us", 128),
                         (DSROW, "ds_row", 1), (MASKS, "masks", 128)):
        cnt = int(np.prod(dst.shape))
        p.wait("gpsimd", ev_c0)
        ev_c0 = p.op("gpsimd", "dma_start", out=dst[:, :],
                     in_=row2d(fpc(nm, cnt), np_), inc="swc", dma=True)
    consts_ev = ev_c0

    p.wait("tensor", consts_ev)
    bcast_st = None
    sc_free = None
    for src, dst in ((BOROW, BOBC), (DSROW, DSBC)):
        p.wait("tensor", sc_free)
        for d in range(4):
            bcast_st = p.op("tensor", "matmul", SC[:, d * 512:(d + 1) * 512],
                            ONES[:, :], src[:, d * 512:(d + 1) * 512],
                            start=True, stop=True,
                            inc=ST if d == 3 else None)
        p.wait("vector", bcast_st)
        sc_free = p.op("vector", "tensor_copy", dst[:, :], SC[:, :], inc=SV)
    p.wait("vector", consts_ev)
    qbs_ev = p.op("vector", "tensor_scalar_mul", QBS,
                  BQKV[:, 0:48].rearrange("p (h t) -> p h t", t=3)[:, :, 0],
                  RS128, inc=SV)

    def ln_stats(src_ap, statbase, ready_ev):
        bst = STAT[:, statbase:statbase + 24]
        mv = STAT[:, statbase + 24:statbase + 26]
        sd = STAT[:, statbase + 26:statbase + 27]
        rstd = STAT[:, statbase + 27:statbase + 28]
        p.wait("vector", ready_ev)
        ev_b = None
        for q in range(4):
            ev_b = p.op("vector", "bn_stats", bst[:, 6 * q:6 * q + 6],
                        src_ap[:, q * 512:(q + 1) * 512],
                        inc=SV if q == 3 else None)
        p.wait("vector", ev_b)
        ev_a = p.op("vector", "bn_aggr", mv, bst, inc=SV)
        p.wait("scalar", ev_a)
        ev_sd = p.op("scalar", "activation", sd, mv[:, 1:2], AF.Sqrt,
                     bias=EPSC, inc=SS)
        p.wait("vector", ev_sd)
        ev_ln = p.op("vector", "reciprocal", rstd, sd, inc=SV)
        return mv[:, 0:1], rstd, ev_ln

    tp_rel = [None, None]
    tp_ctr = [0]

    def transpose16(src_tile, dst, dst_col, dst_stride, norm_ev, tp_bufs,
                    tp_rel_l, copies="alt"):
        ev_t = None
        ev_c = []
        dst3 = dst[:, :].rearrange("p (c n) -> p c n", n=dst_stride)
        for g in range(4):
            b = tp_ctr[0] % 2
            tp_ctr[0] += 1
            p.wait("tensor", [norm_ev, tp_rel_l[b]])
            for cc in range(4):
                c = 4 * g + cc
                ev_t = p.op("tensor", "transpose",
                            tp_bufs[b][:, cc * 128:(cc + 1) * 128],
                            src_tile[:, c * 128:(c + 1) * 128], IDENT[:, :],
                            inc=ST if cc == 3 else None)
            ceng = "scalar" if copies == "scalar" or b == 1 else "vector"
            p.wait(ceng, ev_t)
            ev = p.op(ceng, "copy" if ceng == "scalar" else "tensor_copy",
                      dst3[:, 4 * g:4 * g + 4, dst_col:dst_col + 128],
                      tp_bufs[b][:, 0:512].rearrange("p (c n) -> p c n", n=128),
                      inc=SS if ceng == "scalar" else SV)
            tp_rel_l[b] = ev
            if g >= 2:
                ev_c.append(ev)
        return ev_t, ev_c

    ev_wqkv = {}
    ks_ready = {}
    vs_ready = {}
    q_ready = {}
    ks_rel = [None, None]
    vs_rel = [None, None]
    qh_rel = [None, None]
    vf_rel = [None, None]
    pjp_rel = None
    wqkv_extra = [[], None]
    xhf_tile_ev = {}
    xqf_done = None
    proj_last_mm = [None, None]

    def emit_wqkv_dma(hh):
        par = hh % 2
        p.wait("gpsimd", [proj_last_mm[par], wqkv_extra[par]])
        ev_wqkv[hh] = p.op("gpsimd", "dma_start", out=WQKVB[par][:, :],
                           in_=row2d(fpc(f"wqkv{hh}", 128 * 6144), 128),
                           inc=f"sw{par}", dma=True)

    def emit_proj_unit(hh, u):
        nonlocal vf_rel, pjp_rel
        par = hh % 2
        if u < 4:
            sg = u
            p.wait("tensor", [ev_wqkv[hh], pjp_rel,
                              xhf_tile_ev.get(4 * sg + 3)])
            ev_mm = None
            for c in range(NCH):
                ev_mm = p.op("tensor", "matmul", PJP[:, :],
                             WQKVB[par][:, 2048 + c * 128: 2048 + (c + 1) * 128],
                             XHF[:, c * 2048 + sg * 512: c * 2048 + sg * 512 + 512],
                             start=(c == 0), stop=(c == NCH - 1),
                             inc=ST if c == NCH - 1 else None)
            proj_last_mm[par] = ev_mm
            p.wait("scalar", [ev_mm, ks_rel[par]])
            ev = p.op("scalar", "activation",
                      KS[par][:, sg * 512:(sg + 1) * 512], PJP[:, :],
                      AF.Identity, bias=BQKV[:, 3 * hh + 1:3 * hh + 2], inc=SS)
            pjp_rel = ev
            if sg == 3:
                ks_ready[hh] = ev
        elif u < 8:
            sg = u - 4
            vslot = sg % 2
            p.wait("tensor", [ev_wqkv[hh], pjp_rel,
                              xhf_tile_ev.get(4 * sg + 3)])
            ev_mm = None
            for c in range(NCH):
                ev_mm = p.op("tensor", "matmul", PJP[:, :],
                             WQKVB[par][:, 4096 + c * 128: 4096 + (c + 1) * 128],
                             XHF[:, c * 2048 + sg * 512: c * 2048 + sg * 512 + 512],
                             start=(c == 0), stop=(c == NCH - 1),
                             inc=ST if c == NCH - 1 else None)
            proj_last_mm[par] = ev_mm
            p.wait("scalar", [ev_mm, vf_rel[vslot]])
            ev = p.op("scalar", "activation",
                      VF[vslot][:, :], PJP[:, :],
                      AF.Identity, bias=BQKV[:, 3 * hh + 2:3 * hh + 3], inc=SS)
            pjp_rel = ev
            b = tp_ctr[0] % 2
            tp_ctr[0] += 1
            p.wait("tensor", [ev, tp_rel[b]])
            ev_vt = None
            for q4 in range(4):
                ev_vt = p.op("tensor", "transpose",
                             TP[b][:, q4 * 128:(q4 + 1) * 128],
                             VF[vslot][:, q4 * 128:(q4 + 1) * 128],
                             IDENT[:, :], inc=ST if q4 == 3 else None)
            ceng = "vector" if b == 0 else "scalar"
            p.wait(ceng, [ev_vt, vs_rel[par]])
            ev_c = p.op(ceng, "tensor_copy" if b == 0 else "copy",
                        VS[par][:, sg * 512:(sg + 1) * 512],
                        TP[b][:, 0:512], inc=SV if b == 0 else SS)
            tp_rel[b] = ev_c
            vf_rel[vslot] = ev_vt
            if sg == 3:
                vs_ready[hh] = ev_c
        else:
            tg = u - 8
            p.wait("tensor", [ev_wqkv[hh], pjp_rel, xqf_done])
            ev_mm = None
            for c in range(NCH):
                ev_mm = p.op("tensor", "matmul", PJP[:, :],
                             WQKVB[par][:, c * 128:(c + 1) * 128],
                             XQF[:, c * 1024 + tg * 512: c * 1024 + tg * 512 + 512],
                             start=(c == 0), stop=(c == NCH - 1),
                             inc=ST if c == NCH - 1 else None)
            proj_last_mm[par] = ev_mm
            p.wait("scalar", [ev_mm, qh_rel[par], qbs_ev])
            ev = p.op("scalar", "activation",
                      QH[par][:, tg * 512:(tg + 1) * 512], PJP[:, :],
                      AF.Identity, bias=QBS[:, hh:hh + 1], scale=RS128,
                      inc=SS)
            pjp_rel = ev
            if tg == 1:
                q_ready[hh] = ev

    emit_wqkv_dma(0)
    wo_bf = nc.dram_tensor("wo_bf", [2 * NCH * 128 * 1024], BF16,
                           kind="Internal")
    ev_wobf = p.op("gpsimd", "dma_start",
                   out=wo_bf[:].rearrange("(p n) -> p n", p=2048),
                   in_=fpc("wo0_0", 2 * NCH * 128 * 1024).rearrange(
                       "(p n) -> p n", p=2048),
                   inc="swo", dma=True)

    xt_rel = [bcast_st, bcast_st]
    xtm_rel = [None, None]
    xt_readers = [None, None]
    p.wait("tensor", consts_ev)
    weave = {3: [0], 5: [4], 7: [1], 9: [5], 11: [2], 13: [6], 15: [3, 7]}
    tiles = [("f", i) for i in range(NCH)] + [("q", i) for i in range(QT)]
    pend = None

    def do_transposes(job):
        nonlocal xqf_done
        kind, i, par, ev_norm = job
        if kind == "q":
            ev_t, ev_c = transpose16(XTM[par], XQF, i * 128, 1024, ev_norm,
                                     TP, tp_rel, copies="scalar")
            xqf_done = ev_c
        else:
            ev_t, ev_c = transpose16(XTM[par], XHF, i * 128, 2048, ev_norm,
                                     TP, tp_rel, copies="scalar")
            xhf_tile_ev[i] = ev_c
            for u in weave.get(i, []):
                emit_proj_unit(0, u)
        xtm_rel[par] = ev_t

    for k, (kind, i) in enumerate(tiles):
        par = k % 2
        p.wait("sync", xt_rel[par])
        src_hbm = xq_hbm if kind == "q" else x_hbm
        ev_x = p.op("sync", "dma_start", out=XT[par][:, :],
                    in_=src_hbm[i * 128:(i + 1) * 128, :], inc=f"sx{par}",
                    dma=True)
        mean, rstd, ev_ln = ln_stats(XT[par][:, :], par * 32, ev_x)
        p.wait("vector", [ev_ln, xtm_rel[par]])
        ev_norm = p.op("vector", "tensor_scalar", XTM[par][:, :], XT[par][:, :],
                       mean, rstd, op0=ALU.subtract, op1=ALU.mult, inc=SV)
        xt_rel[par] = [ev_norm]
        xt_readers[par] = [ev_norm]
        if pend is not None:
            do_transposes(pend)
        pend = (kind, i, par, ev_norm)
    do_transposes(pend)
    for u in (8, 9):
        emit_proj_unit(0, u)
    x_done = xt_readers[0] + xt_readers[1] + xtm_rel
    xhf_done = xhf_tile_ev[NCH - 1]
    wqkv_extra[1] = x_done

    tap("XHF", XHF[:, :], [128, NCH * 2048], BF16, [xhf_done])
    tap("XQF", XQF[:, :], [128, NCH * 1024], BF16, [xqf_done])

    sc_rel = None
    sc_half_rel = [None, None]
    os_rel = None
    ps_rel = [None, None]
    pt_rel = [None, None]
    attn_last = None
    pend_tile = [None]

    def emit_tav():
        nonlocal attn_last, os_rel
        if pend_tile[0] is None:
            return
        h, par, j, nb, pbuf, ev_pn = pend_tile[0]
        pend_tile[0] = None
        ev_av = None
        ev_tp_last = None
        pend_av = []

        def emit_av():
            nonlocal ev_av
            g0, gn, bk, ev_psc = pend_av.pop(0)
            p.wait("tensor", ev_psc)
            if g0 == 0:
                p.wait("tensor", [os_rel, vs_ready[h]])
            for cc in range(gn):
                b = g0 + cc
                ev_av = p.op("tensor", "matmul", OP[:, :],
                             VS[par][:, b * 128:(b + 1) * 128],
                             PS[:, bk * 512 + cc * 128:
                                 bk * 512 + (cc + 1) * 128],
                             start=(b == 0), stop=(b == nb - 1),
                             inc=ST if cc == gn - 1 else None)
            ps_rel[bk] = ev_av

        for g0 in range(0, nb, 4):
            gn = min(4, nb - g0)
            bk = tp_ctr[0] % 2
            tp_ctr[0] += 1
            p.wait("tensor", [ev_pn, tp_rel_tp[bk]])
            ev_tp = None
            for cc in range(gn):
                b = g0 + cc
                ev_tp = p.op("tensor", "transpose",
                             TP[bk][:, cc * 128:(cc + 1) * 128],
                             PT[pbuf][:, b * 128:(b + 1) * 128], IDENT[:, :],
                             inc=ST if cc == gn - 1 else None)
            ev_tp_last = ev_tp
            if g0 >= 4:
                emit_av()
            ceng = "vector" if bk == 0 else "scalar"
            p.wait(ceng, [ev_tp, ps_rel[bk]])
            ev_psc = p.op(ceng, "tensor_copy" if bk == 0 else "copy",
                          PS[:, bk * 512:bk * 512 + gn * 128],
                          TP[bk][:, 0:gn * 128], inc=SV if bk == 0 else SS)
            tp_rel_tp[bk] = ev_psc
            pend_av.append((g0, gn, bk, ev_psc))
        emit_av()
        pt_rel[pbuf] = ev_tp_last
        vs_rel[par] = ev_av
        attn_last = ev_av
        p.wait("scalar", ev_av)
        os_rel = p.op("scalar", "activation",
                      OS[:, h * 1024 + j * 128: h * 1024 + (j + 1) * 128],
                      OP[:, :], AF.Copy, inc=SS)

    tp_rel_tp = tp_rel
    for h in range(HEADS):
        par = h % 2
        if h + 1 < HEADS:
            emit_wqkv_dma(h + 1)
        nq = 10 if h + 1 < HEADS else 0
        for j in range(QT):
            ext = 256 * (j + 1)
            nb = 2 * (j + 1)
            sl = 112 + 4 * (j % 2)
            mneg = STAT[:, sl:sl + 1]
            ssum = STAT[:, sl + 1:sl + 2]
            rinv = STAT[:, sl + 2:sl + 3]
            if j <= 3:
                off = (j % 2) * 1024
                p.wait("tensor", [sc_half_rel[j % 2], ks_ready[h],
                                  q_ready[h]])
            else:
                off = 0
                p.wait("tensor", [sc_half_rel[0], sc_half_rel[1], sc_rel,
                                  ks_ready[h], q_ready[h]])
            SCV = SC[:, off:off + ext]
            ev_qk = None
            ev_qk_pre = None
            pre_w = ext - 512 if j > 3 else 0
            for seg in range(0, ext, 512):
                seg_w = min(512, ext - seg)
                last_pre = pre_w and seg + seg_w >= pre_w
                ev = p.op("tensor", "matmul", SCV[:, seg:seg + seg_w],
                          QH[par][:, j * 128:(j + 1) * 128],
                          KS[par][:, seg:seg + seg_w],
                          start=True, stop=True,
                          inc=ST if (seg + seg_w >= ext or last_pre)
                          else None)
                if last_pre:
                    ev_qk_pre = ev
                    pre_w_done, pre_w = pre_w, 0
                ev_qk = ev
            ks_rel[par] = ev_qk
            qh_rel[par] = ev_qk
            if j > 3:
                tmp0 = STAT[:, sl + 3:sl + 4]
                p.wait("vector", ev_qk_pre)
                ev_mx0 = p.op("vector", "reduce_max", tmp0,
                              SCV[:, 0:pre_w_done], axis=AX,
                              negate=True, inc=SV)
            p.wait("vector", ev_qk)
            ev_msk = p.op("vector", "tensor_tensor", SCV[:, ext - 256:ext],
                          SCV[:, ext - 256:ext], MASKS[:, j * 256:(j + 1) * 256],
                          ALU.add, inc=SV)
            p.wait("vector", ev_msk)
            if j > 3:
                ev_mx1 = p.op("vector", "reduce_max", rinv,
                              SCV[:, pre_w_done:ext], axis=AX,
                              negate=True, inc=SV)
                p.wait("vector", ev_mx1)
                ev_mx = p.op("vector", "tensor_tensor", mneg, tmp0, rinv,
                             ALU.min, inc=SV)
            else:
                ev_mx = p.op("vector", "reduce_max", mneg, SCV[:, 0:ext],
                             axis=AX, negate=True, inc=SV)
            pbuf = (h * QT + j) % 2
            p.wait("scalar", [ev_mx, pt_rel[pbuf]])
            ev_exp = p.op("scalar", "activation", PT[pbuf][:, 0:ext],
                          SCV[:, 0:ext],
                          AF.Exp, bias=mneg, scale=1.0, accum_out=ssum,
                          inc=SS)
            if j <= 3:
                sc_half_rel[j % 2] = ev_exp
            else:
                sc_rel = ev_exp
            p.wait("vector", ev_exp)
            ev_ri = p.op("vector", "reciprocal", rinv, ssum, inc=SV)
            p.wait("vector", ev_ri)
            ev_pn = p.op("vector", "tensor_scalar_mul", PT[pbuf][:, 0:ext],
                         PT[pbuf][:, 0:ext], rinv, inc=SV)
            lo = (nq * j) // QT
            hi = (nq * (j + 1)) // QT
            for u in range(lo, hi):
                emit_proj_unit(h + 1, u)
            emit_tav()
            pend_tile[0] = (h, par, j, nb, pbuf, ev_pn)
    emit_tav()
    os_done = os_rel
    xhf_rel = [proj_last_mm[0], proj_last_mm[1]]

    tap("OS", OS[:, :], [128, HEADS * 1024], BF16, [os_done])

    nc.psum_base = psum_mark
    TP2 = [nc.alloc_psum_tensor(f"TP2_{i}", [128, 1024], BF16)
           for i in range(2)]
    tp_rel2 = [[os_done] + tp_rel + [pjp_rel], [os_done] + tp_rel + [pjp_rel]]

    ev_xq = []
    p.wait("gpsimd", xhf_rel)
    p.wait("sync", list(tap_evs))
    p.wait("gpsimd", list(tap_evs))
    for j in range(QT):
        if j:
            p.wait("gpsimd", ev_xq[-1])
        ev_xq.append(p.op("gpsimd", "dma_start",
                          out=YS[:, j * 2048:(j + 1) * 2048],
                          in_=xq_hbm[j * 128:(j + 1) * 128, :], inc="sxq",
                          dma=True))

    pre_u8 = [None, None]
    for par8 in range(2):
        if par8 == 0:
            p.wait("scalar", xhf_rel)
        pre_u8[par8] = p.op(
            "scalar", "dma_start", out=GUD8[par8][:, :],
            in_=row2d(ub_hbm[par8 * U8_CHUNK:(par8 + 1) * U8_CHUNK], 128),
            inc=f"su{par8}", dma=True)

    woc_rel = [[attn_last], [attn_last]]
    pj_rel = [[sc_rel, sc_half_rel[0], sc_half_rel[1]],
              [sc_rel, sc_half_rel[0], sc_half_rel[1]]]
    ev_y = {}
    ytm_rel = [None, None]
    yf_done = None
    for jq in range(4):
        for dh in range(2):
            ev_wo_j = [None, None]
            for c in range(NCH):
                cb = (c // 4) % 2
                if c % 4 == 0:
                    qeng = "sync" if cb == 0 else "gpsimd"
                    p.wait(qeng, [woc_rel[cb], ev_wobf])
                    if cb and ev_xq:
                        p.wait(qeng, ev_xq[-1])
                    ev_woc = p.op(
                        qeng, "dma_start",
                        out=WOC[cb][:, :].rearrange("p (c n) -> p c n", c=4),
                        in_=wo_bf[((dh * NCH + c) * 128) * 1024:
                                  ((dh * NCH + c + 4) * 128) * 1024
                                  ].rearrange("(c p n) -> p c n", c=4, p=128),
                        inc="sx0" if cb == 0 else "swoc", dma=True)
                p.wait("tensor", ev_woc)
                if c == 0:
                    p.wait("tensor", [os_done, pj_rel[0], pj_rel[1]])
                ev_mm = None
                for jl in range(2):
                    j = 2 * jq + jl
                    PJ = SC[:, jl * 1024:(jl + 1) * 1024]
                    for d in range(2):
                        last_c = (c == NCH - 1)
                        do_inc = (last_c and d == 1) or (jl == 1 and d == 1)
                        ev_mm = p.op(
                            "tensor", "matmul", PJ[:, d * 512:(d + 1) * 512],
                            OS[:, c * 1024 + j * 128:
                                c * 1024 + (j + 1) * 128],
                            WOC[cb][:, (c % 4) * 1024 + d * 512:
                                (c % 4) * 1024 + (d + 1) * 512],
                            start=(c == 0), stop=last_c,
                            inc=ST if do_inc else None)
                        if last_c and d == 1:
                            ev_wo_j[jl] = ev_mm
                woc_rel[cb] = ev_mm
            for jl in range(2):
                j = 2 * jq + jl
                PJ = SC[:, jl * 1024:(jl + 1) * 1024]
                ysl = YS[:, j * 2048 + dh * 1024: j * 2048 + dh * 1024 + 1024]
                p.wait("vector", [ev_wo_j[jl], ev_xq[j]])
                ev0 = p.op("vector", "scalar_tensor_tensor", ysl, PJ, 0.0, ysl,
                           op0=ALU.add, op1=ALU.add, inc=SV)
                p.wait("vector", ev0)
                ev = p.op("vector", "tensor_tensor", ysl, ysl,
                          BOBC[:, dh * 1024: dh * 1024 + 1024], ALU.add,
                          inc=SV)
                pj_rel[jl] = ev
                ev_y[(j, dh)] = ev
        for jl in range(2):
            j = 2 * jq + jl
            par = j % 2
            ytile = YS[:, j * 2048:(j + 1) * 2048]
            ready = [ev_y[(j, 0)], ev_y[(j, 1)]]
            mean, rstd, ev_ln = ln_stats(ytile, 56 + par * 28, ready)
            p.wait("vector", [ev_ln, ytm_rel[par]])
            ev_norm = p.op("vector", "tensor_scalar", YTM[par][:, :], ytile,
                           mean, rstd, op0=ALU.subtract, op1=ALU.mult, inc=SV)
            ev_t, ev_c = transpose16(YTM[par], YF, j * 128, 1024, ev_norm,
                                     TP2, tp_rel2)
            ytm_rel[par] = ev_t
            yf_done = ev_c
    wo_done_t = [woc_rel[0], woc_rel[1]]

    tap("YS_W", YS[:, :], [128, QT * 2048], F32,
        [ev_y[(j, dh)] for j in range(QT) for dh in range(2)])
    tap("YF", YF[:, :], [128, NCH * 1024], BF16, [yf_done])

    nc.psum_base = psum_mark
    PB2 = nc.alloc_psum_tensor("PB2", [128, 2048], F32)
    GP = PB2[:, 0:1024]
    UP = PB2[:, 1024:2048]

    gud_rel = [None, None]
    gd_rel = ud_rel = None
    ddg_rel = None
    sig_rel = None
    hbuf_rel = [None] * IGR
    up_rel = None
    dpsum_rel = {0: [pj_rel[0], pj_rel[1]],
                 1: [yf_done, tp_rel2[0], tp_rel2[1]]}
    ev_h_grp = None
    final_evac = {}

    p.wait("sync", list(tap_evs))
    p.wait("scalar", list(tap_evs))
    p.wait("vector", list(tap_evs))
    for g in range(IGR):
        ev_dd = None
        for cc in range(IGR):
            ic = g * IGR + cc
            par = ic % 2
            if ic < 2:
                ev_u8 = pre_u8[ic]
            else:
                p.wait("sync", gud_rel[par])
                ev_u8 = p.op("sync", "dma_start", out=GUD8[par][:, :],
                             in_=row2d(ub_hbm[ic * U8_CHUNK:
                                              (ic + 1) * U8_CHUNK], 128),
                             inc=f"su{par}", dma=True)
            p.wait("scalar", [ev_u8, gd_rel])
            if ic == 0:
                p.wait("scalar", wo_done_t)
                p.wait("vector", wo_done_t)
            ev_gd = p.op("scalar", "activation", GD[:, :],
                         GUD8[par][:, 0:2048], AF.Copy, bias=-128.0, inc=SS)
            p.wait("scalar", ud_rel)
            ev_ud = p.op("scalar", "activation", UD[:, :],
                         GUD8[par][:, 2048:4096], AF.Copy, bias=-128.0, inc=SS)
            p.wait("vector", [ev_u8, ddg_rel])
            ev_dd = p.op("vector", "scalar_tensor_tensor",
                         DDG[:, cc * 2048:(cc + 1) * 2048],
                         GUD8[par][:, 4096:6144], -128.0, DSBC[:, :],
                         op0=ALU.add, op1=ALU.mult, inc=SV)
            gud_rel[par] = [ev_ud, ev_dd]
            p.wait("tensor", [ev_gd, sig_rel, yf_done, dpsum_rel[1]])
            ev_gm = None
            for c in range(NCH):
                for tg in range(2):
                    last = (c == NCH - 1 and tg == 1)
                    ev_gm = p.op("tensor", "matmul",
                                 GP[:, tg * 512:(tg + 1) * 512],
                                 GD[:, c * 128:(c + 1) * 128],
                                 YF[:, c * 1024 + tg * 512:
                                     c * 1024 + tg * 512 + 512],
                                 start=(c == 0), stop=(c == NCH - 1),
                                 inc=ST if last else None)
            gd_rel = ev_gm
            p.wait("tensor", [ev_ud, up_rel])
            ev_um = None
            for c in range(NCH):
                for tg in range(2):
                    last = (c == NCH - 1 and tg == 1)
                    ev_um = p.op("tensor", "matmul",
                                 UP[:, tg * 512:(tg + 1) * 512],
                                 UD[:, c * 128:(c + 1) * 128],
                                 YF[:, c * 1024 + tg * 512:
                                     c * 1024 + tg * 512 + 512],
                                 start=(c == 0), stop=(c == NCH - 1),
                                 inc=ST if last else None)
            ud_rel = ev_um
            p.wait("scalar", [ev_gm, up_rel])
            sig_rel = p.op("scalar", "activation", SIG[:, :], GP[:, :],
                           AF.Silu, scale=GSC[:, ic:ic + 1], inc=SS)
            p.wait("vector", [sig_rel, ev_um, hbuf_rel[cc]])
            ev_h = p.op("vector", "scalar_tensor_tensor",
                        HBUF[:, cc * 1024:(cc + 1) * 1024], UP[:, :],
                        USC[:, ic:ic + 1], SIG[:, :],
                        op0=ALU.mult, op1=ALU.mult, inc=SV)
            up_rel = ev_h
            if cc == IGR - 1:
                ev_h_grp = ev_h
        if g == 0:
            tap("HBUF0", HBUF[:, :], [128, IGR * 1024], BF16, [ev_h_grp])
            tap("DDG0", DDG[:, :], [128, IGR * 2048], BF16, [ev_dd])
            if debug_taps:
                p.wait("sync", list(tap_evs))
                p.wait("scalar", list(tap_evs))
                p.wait("vector", list(tap_evs))
                p.wait("gpsimd", list(tap_evs))
        ev_dn_last = None
        for j in range(QT):
            DP = SC if j % 2 == 0 else PB2
            p.wait("tensor", [ev_h_grp, ev_dd, dpsum_rel[j % 2]])
            ev_mm = None
            for cc2 in range(IGR):
                for d in range(4):
                    last = (cc2 == IGR - 1 and d == 3)
                    ev_mm = p.op(
                        "tensor", "matmul", DP[:, d * 512:(d + 1) * 512],
                        HBUF[:, cc2 * 1024 + j * 128:
                             cc2 * 1024 + (j + 1) * 128],
                        DDG[:, cc2 * 2048 + d * 512:
                            cc2 * 2048 + d * 512 + 512],
                        start=(cc2 == 0), stop=(cc2 == IGR - 1),
                        inc=ST if last else None)
            ev_dn_last = ev_mm
            p.wait("vector", [ev_mm, dpsum_rel[j % 2],
                              ev_y[(j, 0)], ev_y[(j, 1)]])
            ev_ev = p.op("vector", "tensor_tensor",
                         YS[:, j * 2048:(j + 1) * 2048],
                         YS[:, j * 2048:(j + 1) * 2048], DP[:, :], ALU.add,
                         inc=SV)
            dpsum_rel[j % 2] = ev_ev
            if g == IGR - 1:
                final_evac[j] = ev_ev
        ddg_rel = ev_dn_last
        for cc2 in range(IGR):
            hbuf_rel[cc2] = ev_dn_last

    ev_out = None
    for j in range(QT):
        p.wait("gpsimd", final_evac[j])
        ev_out = p.op("gpsimd", "dma_start",
                      out=out_hbm[j * 128:(j + 1) * 128, :],
                      in_=YS[:, j * 2048:(j + 1) * 2048], inc=SO, dma=True)
    p.wait("gpsimd", ev_out)

    p.build()
    return nc, taps


def kernel(**inputs):
    cores = pack_host(inputs)
    nc, _ = build_nc()
    in_maps = [dict(x_full=c["x_full"], x_q=c["x_q"], f32_blob=c["f32_blob"],
                    u8_blob=c["u8_blob"]) for c in cores]
    res = run_bass_kernel_spmd(nc, in_maps, core_ids=list(range(8)))
    out = np.empty((4, 2048, 2048), dtype=np.float32)
    for c in range(8):
        out[c // 2][_q_rows(c % 2)] = res.results[c]["out"]
    return out
